# version 19 (final)
# baseline (speedup 1.0000x reference)
import numpy as np
import ml_dtypes
import concourse.bass as bass
import concourse.mybir as mybir
import concourse.library_config as libcfg
from concourse.bass_utils import run_bass_kernel_spmd
from concourse.bass_core_rust import IndirectOffsetOnAxis
from contextlib import ExitStack

F32 = mybir.dt.float32
BF16 = mybir.dt.bfloat16
I16 = mybir.dt.int16
U32 = mybir.dt.uint32
A = mybir.AluOpType
ACT = mybir.ActivationFunctionType
AX = mybir.AxisListType.X

B, S, H, NH, HD = 8, 2048, 2048, 16, 128
E, CAP, HHD = 8, 2048, 2816
NHT, NST, NHHT = 16, 16, 22
INV_SQRT_HD = float(1.0 / np.sqrt(np.float32(128.0)))
EXP_SHIFT = -4.0
EPS = 1e-5
CVALS = [0.0, 1.0, 2.0, 3.0, 4.0, 5.0, 6.0, 7.0,
         -1.0, EXP_SHIFT, float(CAP), 16383.0, EPS, 1024.0]


class Ch:
    def __init__(self, nc, ctx, name):
        self.sem = ctx.enter_context(nc.semaphore(name))
        self.n = 0

    def inc(self, inst, k=1):
        self.n += k
        inst.then_inc(self.sem, k)
        return self.n



def _ring2(En, nc, name, width, dt):
    buf = En(nc.sbuf_tensor(name, [128, 2 * width], dt))
    return [buf[:, 0:width], buf[:, width:2 * width]]

def build(debug=False, stop_after=None, shared_ag=True):
    nc = bass.Bass("TRN2", target_bir_lowering=False)
    dram = nc.dram_tensor

    xT = dram("xT", [B, H, S], F32, kind="ExternalInput")
    x_nat = dram("x_nat", [S, H], F32, kind="ExternalInput")
    xTc = dram("xTc", [NST, 128, H], F32, kind="ExternalInput")
    wqkv_d = dram("wqkv", [128, NHT * 6 * 128], F32, kind="ExternalInput")
    wprojT_d = dram("wprojT", [NHT, 128, H], F32, kind="ExternalInput")
    wproj_nat = dram("wproj_nat", [NHT, 128, H], F32, kind="ExternalInput")
    bprojr_d = dram("bprojr", [1, H], F32, kind="ExternalInput")
    bprojc_d = dram("bprojc", [128, NHT], F32, kind="ExternalInput")
    wg_d = dram("wg", [128, NHT * E], F32, kind="ExternalInput")
    w1_d = dram("w1", [NHHT, 128, NHT * 128], F32, kind="ExternalInput")
    w3_d = dram("w3", [NHHT, 128, NHT * 128], F32, kind="ExternalInput")
    w2_d = dram("w2", [NHHT, 128, H], F32, kind="ExternalInput")
    sw1_d = dram("sw1", [NHHT, 128, NHT * 128], F32, kind="ExternalInput")
    sw3_d = dram("sw3", [NHHT, 128, NHT * 128], F32, kind="ExternalInput")
    sw2_d = dram("sw2", [NHHT, 128, H], F32, kind="ExternalInput")
    sel_d = dram("sel16", [128, NST], F32, kind="ExternalInput")
    myexp_d = dram("myexp", [128, 1], F32, kind="ExternalInput")
    y = dram("y", [S, H], F32, kind="ExternalOutput")

    x1_stage = dram("x1_stage", [S, H], BF16)
    x1_all = dram("x1_all", [B * S, H], BF16, addr_space="Shared" if shared_ag else "Local")
    eidx_stage = dram("eidx_stage", [16, 128], F32)
    eidx_all = dram("eidx_all", [128, 128], F32, addr_space="Shared" if shared_ag else "Local")
    oe_stage = dram("oe_stage", [CAP, H], BF16)
    oe_all = dram("oe_all", [E * CAP, H], BF16, addr_space="Shared" if shared_ag else "Local")
    oes_stage = dram("oes_stage", [S, H], BF16)
    floc_hbm = dram("floc_hbm", [128, 128], F32)
    w1b = dram("w1b", [NHHT, 128, NHT * 128], BF16)
    w3b = dram("w3b", [NHHT, 128, NHT * 128], BF16)
    w2b = dram("w2b", [NHHT, 128, H], BF16)
    sw1b = dram("sw1b", [NHHT, 128, NHT * 128], BF16)
    sw3b = dram("sw3b", [NHHT, 128, NHT * 128], BF16)
    sw2b = dram("sw2b", [NHHT, 128, H], BF16)

    dbg = {}
    if debug:
        for nm, shp in [("ctxT", [128, NHT * S]), ("x1", [S, H]),
                        ("eidx", [16, 128]), ("gate", [128, NST]),
                        ("pos", [128, 128]), ("keep", [128, 128]),
                        ("stok", [128, 16]), ("gk", [128, NST]),
                        ("reti", [128, NST]), ("xd", [128, H]),
                        ("oes", [S, H]), ("oe", [CAP, H]),
                        ("eg", [128, 128])]:
            dbg[nm] = dram("dbg_" + nm, shp, F32, kind="ExternalOutput")

    id_f = nc.inline_tensor(np.eye(128, dtype=np.float32), "id_f")
    id_b = nc.inline_tensor(np.eye(128, dtype=ml_dtypes.bfloat16), "id_b")
    onescol_f = nc.inline_tensor(np.ones((128, 1), dtype=np.float32), "onescol_f")
    onescol_b = nc.inline_tensor(np.ones((128, 1), dtype=ml_dtypes.bfloat16), "onescol_b")
    onesrow_f = nc.inline_tensor(np.ones((1, 128), dtype=np.float32), "onesrow_f")
    tri_d = nc.inline_tensor(np.triu(np.ones((128, 128), dtype=np.float32), 1), "tri")
    shift1_d = nc.inline_tensor(np.eye(128, k=1).astype(np.float32), "shift1")
    iota8_d = nc.inline_tensor(
        np.tile(np.arange(8, dtype=np.float32)[None, :], (128, 1)), "iota8")
    big8_d = nc.inline_tensor(np.full((128, 8), 64.0, dtype=np.float32), "big8")
    iota8w_d = nc.inline_tensor(
        np.tile(np.arange(8, dtype=np.float32)[None, :], (128, 16)), "iota8w")
    big8w_d = nc.inline_tensor(np.full((128, 128), 64.0, dtype=np.float32), "big8w")
    consts_d = nc.inline_tensor(
        np.tile(np.array(CVALS, dtype=np.float32)[None, :], (128, 1)), "cvals")

    ctx = ExitStack()
    with ctx:
        En = ctx.enter_context

        s_consts = En(nc.sbuf_tensor("s_consts", [128, len(CVALS)], F32))
        for j, v in enumerate(CVALS):
            nc.const_aps.register(F32, v, s_consts[:, j:j + 1])
        s_idf = En(nc.sbuf_tensor("s_idf", [128, 128], F32))
        s_idb = En(nc.sbuf_tensor("s_idb", [128, 128], BF16))
        s_1cf = En(nc.sbuf_tensor("s_1cf", [128, 1], F32))
        s_1cb = En(nc.sbuf_tensor("s_1cb", [128, 1], BF16))
        s_1rf = En(nc.sbuf_tensor("s_1rf", [1, 128], F32))
        s_tri = En(nc.sbuf_tensor("s_tri", [128, 128], F32))
        s_io8 = En(nc.sbuf_tensor("s_io8", [128, 8], F32))
        s_big8 = En(nc.sbuf_tensor("s_big8", [128, 8], F32))
        s_sel = En(nc.sbuf_tensor("s_sel", [128, NST], F32))
        s_mye = En(nc.sbuf_tensor("s_mye", [128, 1], F32))
        s_wg = En(nc.sbuf_tensor("s_wg", [128, NHT * E], F32))
        s_bprojc = En(nc.sbuf_tensor("s_bprojc", [128, NHT], F32))
        s_G = En(nc.sbuf_tensor("s_G", [128, 8], F32))
        s_bwg = En(nc.sbuf_tensor("s_bwg", [128, 8], F32))
        s_wpb = En(nc.sbuf_tensor("s_wpb", [128, NHT * 8], BF16))
        s_gate = En(nc.sbuf_tensor("s_gate", [128, NST], F32))
        s_eidxb = En(nc.sbuf_tensor("s_eidxb", [128, NST], F32))
        s_gk = En(nc.sbuf_tensor("s_gk", [128, NST], F32))
        s_retiU = En(nc.sbuf_tensor("s_retiU", [128, NST], U32))
        s_stokU = En(nc.sbuf_tensor("s_stokU", [128, 16], U32))
        s_bn = En(nc.sbuf_tensor("s_bn", [128, 24], F32))
        s_mv = En(nc.sbuf_tensor("s_mv", [128, 2], F32))
        s_sig = En(nc.sbuf_tensor("s_sig", [128, 1], F32))
        s_rsig = En(nc.sbuf_tensor("s_rsig", [128, 1], F32))
        s_nmr = En(nc.sbuf_tensor("s_nmr", [128, 1], F32))
        s_rsig16 = En(nc.sbuf_tensor("s_rsig16", [128, 16], F32))
        s_nmr16 = En(nc.sbuf_tensor("s_nmr16", [128, 16], F32))
        s_tmp8r = En(nc.sbuf_tensor("s_tmp8r", [1, 8], F32))

        pbA = En(nc.psum_tensor("pbA", [128, 2048], F32))
        pbB = En(nc.psum_tensor("pbB", [128, 2048], F32))
        pbAb = pbA.ap().bitcast(BF16)
        pbBb = pbB.ap().bitcast(BF16)

        class _PSl:
            def __init__(self, t, c0, w):
                self.t, self.c0, self.w = t, c0, w

            def __getitem__(self, idx):
                p, f = idx
                a = f.start if f.start is not None else 0
                b = f.stop if f.stop is not None else self.w
                return self.t[p, self.c0 + a:self.c0 + b]

        pb = [_PSl(pbA if k < 4 else pbB, (k % 4) * 512, 512) for k in range(8)]
        pbb = [_PSl(pbAb if k < 4 else pbBb, (k % 4) * 1024, 1024) for k in range(8)]

        chnames = ["ld0", "wq", "xt", "wp", "wpc", "m1", "m1e", "m2", "exp",
                   "m3", "dcp", "rcp", "bc", "bce", "ctx", "xtc", "xres",
                   "m4", "tt", "m4r", "rt8", "lnv", "sqc", "x1v", "x1w",
                   "eix", "agx", "age", "eir", "rtv", "rtt", "rte", "stok",
                   "gx", "gx2", "w2s", "x1r", "tpo", "tpe", "w13", "w2l", "m56", "h1",
                   "m7", "sil", "oev", "oew", "agoe", "ret", "x1r2", "oesr",
                   "yv", "yw", "dbgw", "dbgg", "gexp", "xrdy", "dsum", "dsug",
                   "xt0", "xt1", "wpn0", "wpn1", "xres0", "xres1",
                   "xtc0", "xtc1", "x1w0", "x1w1", "x1rA", "x1rB",
                   "w13a", "w13b", "oewA", "oewB", "retA", "retB", "wcast",
                   "x1fA", "x1fB", "oesA", "oesB", "ywA", "ywB"]
        C_ = {name: Ch(nc, ctx, "ch_" + name) for name in chnames}

        sy, gp, te, ve, sc = nc.sync, nc.gpsimd, nc.tensor, nc.vector, nc.scalar

        for buf, src in [(s_consts, consts_d), (s_idf, id_f), (s_idb, id_b),
                         (s_1cf, onescol_f), (s_1cb, onescol_b), (s_1rf, onesrow_f),
                         (s_tri, tri_d), (s_io8, iota8_d), (s_big8, big8_d)]:
            C_["ld0"].inc(sy.dma_start(out=buf[:], in_=src.ap()), 16)
        for buf, src in [(s_sel, sel_d), (s_mye, myexp_d), (s_wg, wg_d),
                         (s_bprojc, bprojc_d)]:
            C_["ld0"].inc(sy.dma_start(out=buf[:], in_=src[:, :]), 16)
        ld0_done = C_["ld0"].n

        p0ctx = ctx.enter_context(ExitStack())
        E0 = p0ctx.enter_context
        s_wpT = E0(nc.sbuf_tensor("s_wpT", [8, H], F32))
        s_wpn = _ring2(E0, nc, "s_wpn", H, F32)

        wpn_at = []
        for t in range(NHT):
            if t >= 2:
                sy.wait_ge(C_["wpc"].sem, t - 1)
            ch = C_["wpn%d" % (t % 2)]
            wpn_at.append((ch, ch.inc(sy.dma_start(
                out=s_wpn[t % 2][:, :], in_=wproj_nat[t]), 16)))

        te.wait_ge(C_["ld0"].sem, ld0_done)
        for t in range(NHT):
            te.wait_ge(wpn_at[t][0].sem, wpn_at[t][1])
            for cc in range(4):
                mm_i = te.matmul(pb[cc][0:8, :], s_wg[:, t * 8:(t + 1) * 8],
                                 s_wpn[t % 2][:, cc * 512:(cc + 1) * 512],
                                 start=(t == 0), stop=(t == NHT - 1))
            C_["wpc"].inc(mm_i)
        for t in range(NHT):
            te.matmul(pb[4][0:1, 0:8], s_1cf[:], s_wg[:, t * 8:(t + 1) * 8],
                      start=(t == 0), stop=(t == NHT - 1))
        for t in range(NHT):
            mm_i = te.matmul(pb[4][0:1, 8:16], s_bprojc[:, t:t + 1],
                             s_wg[:, t * 8:(t + 1) * 8],
                             start=(t == 0), stop=(t == NHT - 1))
        m4c = C_["m4"].inc(mm_i)

        sc.wait_ge(C_["m4"].sem, m4c)
        for cc in range(4):
            sc_i = sc.copy(s_wpT[:, cc * 512:(cc + 1) * 512], pb[cc][0:8, :])
        sq1 = C_["sqc"].inc(sc_i)
        te.wait_ge(C_["sqc"].sem, sq1)
        for t in range(NHT):
            mm_i = te.transpose(pb[5][:, t * 8:(t + 1) * 8],
                                s_wpT[0:8, t * 128:(t + 1) * 128], s_idf[0:8, 0:8])
        m4c = C_["m4"].inc(mm_i)
        sc.wait_ge(C_["m4"].sem, m4c)
        sc.copy(s_wpb[:], pb[5][:, 0:NHT * 8])
        sq2 = C_["sqc"].inc(sc.copy(s_tmp8r[:], pb[4][0:1, 0:8]))
        te.wait_ge(C_["sqc"].sem, sq2)
        m4c = C_["m4"].inc(te.matmul(pb[6][:, 0:8], s_1rf[0:1, :],
                                      s_tmp8r[0:1, :], start=True, stop=True))
        sc.wait_ge(C_["m4"].sem, m4c)
        sc.copy(s_G[:], pb[6][:, 0:8])
        sq3 = C_["sqc"].inc(sc.copy(s_tmp8r[:], pb[4][0:1, 8:16]))
        te.wait_ge(C_["sqc"].sem, sq3)
        m4c = C_["m4"].inc(te.matmul(pb[6][:, 8:16], s_1rf[0:1, :],
                                      s_tmp8r[0:1, :], start=True, stop=True))
        sc.wait_ge(C_["m4"].sem, m4c)
        C_["sqc"].inc(sc.copy(s_bwg[:], pb[6][:, 8:16]))

        nc.all_engine_barrier()
        p0ctx.close()

        p12ctx = ctx.enter_context(ExitStack())
        s_ctxT = p12ctx.enter_context(nc.sbuf_tensor("s_ctxT", [128, NHT * S], BF16))
        p1ctx = ctx.enter_context(ExitStack())
        E1 = p1ctx.enter_context
        s_wqkv = E1(nc.sbuf_tensor("s_wqkv", [128, NHT * 6 * 128], BF16))
        s_xth = _ring2(E1, nc, "s_xth", NHT * 512, BF16)
        s_qkT = E1(nc.sbuf_tensor("s_qkT", [128, 4 * S], BF16))
        s_vb = E1(nc.sbuf_tensor("s_vb", [128, NHT * 256], BF16))
        s_expT = _ring2(E1, nc, "s_expT", NHT * 512, BF16)
        s_den = [E1(nc.sbuf_tensor("s_denA", [1, 512], F32)),
                 E1(nc.sbuf_tensor("s_denB", [1, 512], F32))]
        s_recip = [E1(nc.sbuf_tensor("s_recipA", [1, 512], F32)),
                   E1(nc.sbuf_tensor("s_recipB", [1, 512], F32))]
        s_rb = E1(nc.sbuf_tensor("s_rb", [128, 1024], F32))

        C_["wq"].inc(gp.dma_start(out=s_wqkv[:], in_=wqkv_d[:, :]), 16)

        xt_at = {}
        m1_free = {}

        cast_map = {"w1": (w1_d, w1b), "w3": (w3_d, w3b), "w2": (w2_d, w2b),
                    "sw1": (sw1_d, sw1b), "sw3": (sw3_d, sw3b),
                    "sw2": (sw2_d, sw2b)}
        cast_jobs = []
        for hhb in (0, 1):
            cast_jobs += [("w1", hhb), ("w3", hhb)]
        cast_jobs += [("w2", hhb) for hhb in range(NHHT)]
        for hhb in range(2, NHHT):
            cast_jobs += [("w1", hhb), ("w3", hhb)]
        for hhb in (0, 1):
            cast_jobs += [("sw1", hhb), ("sw3", hhb)]
        cast_jobs += [("sw2", hhb) for hhb in range(NHHT)]
        for hhb in range(2, NHHT):
            cast_jobs += [("sw1", hhb), ("sw3", hhb)]
        cast_pos = [0]
        wcast_final = 16 * len(cast_jobs)

        def drip_casts(k):
            for _ in range(k):
                if cast_pos[0] >= len(cast_jobs):
                    return
                nm, hhb = cast_jobs[cast_pos[0]]
                cast_pos[0] += 1
                fsrc, bdst = cast_map[nm]
                C_["wcast"].inc(gp.dma_start(out=bdst[hhb], in_=fsrc[hhb]), 16)

        def issue_xt(gq):
            par = gq % 2
            if gq - 2 >= 0 and (gq - 2) in m1_free:
                gp.wait_ge(C_["m1e"].sem, m1_free[gq - 2])
            j, qf = gq // 4, gq % 4
            ch = C_["xt%d" % par]
            for ht in range(NHT):
                ch.inc(gp.dma_start(
                    out=s_xth[par][:, ht * 512:(ht + 1) * 512],
                    in_=xT[j, ht * 128:(ht + 1) * 128, qf * 512:(qf + 1) * 512]), 16)
            xt_at[gq] = (ch, ch.n)

        issue_xt(0)
        issue_xt(1)
        drip_casts(8)
        te.wait_ge(C_["wq"].sem, 16)

        expT_reader = [0, 0]
        dacc_mm = [0, 0]
        ctx_bank_user = [0, 0]
        bce_user = [0, 0]

        pend = None

        def emit_bcast_ctx(p):
            eb_, ip_, qc_, dcp_n_ = p
            db_ = eb_ * 2 + 1
            te.wait_ge(C_["dcp"].sem, dcp_n_)
            bcc = C_["bc"].inc(te.matmul(
                pb[db_][:, :], s_1rf[0:1, :], s_recip[eb_][:, :],
                start=True, stop=True))
            sc.wait_ge(C_["bc"].sem, bcc)
            bce_user[eb_] = C_["bce"].inc(sc.copy(
                s_rb[:, eb_ * 512:eb_ * 512 + 512], pb[db_][:, :]))
            ve.wait_ge(C_["bce"].sem, bce_user[eb_])
            ctx_bank_user[eb_] = C_["ctx"].inc(ve.tensor_tensor(
                s_ctxT[:, ip_ * S + qc_ * 512: ip_ * S + (qc_ + 1) * 512],
                pb[eb_ * 2][:, :], s_rb[:, eb_ * 512:eb_ * 512 + 512], A.mult))

        pend_m3 = None

        def emit_prev_chunk(p, kg):
            eb_p, hh_p, ip_p, qc_p, expd = p
            cb_p, db_p = eb_p * 2, eb_p * 2 + 1
            if kg == 0:
                if ctx_bank_user[eb_p] > 0:
                    te.wait_ge(C_["ctx"].sem, ctx_bank_user[eb_p])
                te.wait_ge(C_["exp"].sem, expd)
            if kg < 4:
                for kt in range(kg * 4, kg * 4 + 4):
                    mm_i = te.matmul(
                        pb[cb_p][:, :],
                        s_vb[:, kt * 256 + hh_p * 128: kt * 256 + hh_p * 128 + 128],
                        s_expT[eb_p][:, kt * 512:(kt + 1) * 512],
                        start=(kt == 0), stop=(kt == NHT - 1))
                if kg == 3:
                    expT_reader[eb_p] = C_["m3"].inc(mm_i)
            else:
                if kg == 4 and bce_user[eb_p] > 0:
                    te.wait_ge(C_["bce"].sem, bce_user[eb_p])
                for kt in range((kg - 4) * 4, (kg - 4) * 4 + 4):
                    mm_i = te.matmul(pb[db_p][0:1, :], s_1cb[:],
                                     s_expT[eb_p][:, kt * 512:(kt + 1) * 512],
                                     start=(kt == 0), stop=(kt == NHT - 1))
                if kg == 7:
                    dacc_mm[eb_p] = C_["rtt"].inc(mm_i)

        def finish_prev(p):
            nonlocal pend
            eb_p, hh_p, ip_p, qc_p, expd = p
            db_p = eb_p * 2 + 1
            sc.wait_ge(C_["rtt"].sem, dacc_mm[eb_p])
            sc.activation(s_den[eb_p][:, :], pb[db_p][0:1, :], ACT.Ln)
            sc.drain()
            dcp_n = C_["dcp"].inc(sc.activation(
                s_recip[eb_p][:, :], s_den[eb_p][:, :], ACT.Exp, scale=-1.0))
            if pend is not None:
                emit_bcast_ctx(pend)
            pend = (eb_p, ip_p, qc_p, dcp_n)

        m1_groups = 0
        m1e_done = 0
        for j in range(B):
            for qf in range(4):
                gq = j * 4 + qf
                te.wait_ge(xt_at[gq][0].sem, xt_at[gq][1])
                xb = s_xth[gq % 2]
                te.wait_ge(C_["m1e"].sem, m1e_done)
                if qf == 0:
                    te.wait_ge(C_["ctx"].sem, C_["ctx"].n)
                    te.wait_ge(C_["bce"].sem, C_["bce"].n)
                for fi in range(4):
                    for ht in range(NHT):
                        w_ap = s_wqkv[:, (ht * 6 + fi) * 128:(ht * 6 + fi + 1) * 128]
                        mm_i = te.matmul(pb[fi][:, :], w_ap,
                                         xb[:, ht * 512:(ht + 1) * 512],
                                         start=(ht == 0), stop=(ht == NHT - 1))
                m1c = C_["m1"].inc(mm_i)
                ve.wait_ge(C_["m1"].sem, m1c)
                for fi in range(4):
                    ve.tensor_copy(s_qkT[:, fi * S + qf * 512:(fi * S + (qf + 1) * 512)],
                                   pb[fi][:, :])
                ve.drain()
                m1e_done = C_["m1e"].inc(ve.nop())
                if qf == 0:
                    te.wait_ge(C_["exp"].sem, C_["exp"].n)
                for tt in range(4):
                    for ht in range(NHT):
                        w_ap = s_wqkv[:, (ht * 6 + 4) * 128:(ht * 6 + 6) * 128]
                        mm_i = te.matmul(pb[4 + tt // 2][:, (tt % 2) * 256:(tt % 2) * 256 + 256],
                                         xb[:, ht * 512 + tt * 128: ht * 512 + (tt + 1) * 128],
                                         w_ap, start=(ht == 0), stop=(ht == NHT - 1))
                m1c = C_["m1"].inc(mm_i)
                ve.wait_ge(C_["m1"].sem, m1c)
                for tt in range(4):
                    gt = qf * 4 + tt
                    ve.tensor_copy(s_vb[:, gt * 256:(gt + 1) * 256],
                                   pb[4 + tt // 2][:, (tt % 2) * 256:(tt % 2) * 256 + 256])
                ve.drain()
                m1_free[gq] = C_["m1e"].inc(ve.nop())
                m1e_done = m1_free[gq]
                if gq + 2 < B * 4:
                    issue_xt(gq + 2)
                drip_casts(4)

            te.wait_ge(C_["m1e"].sem, C_["m1e"].n)

            for hh in range(2):
                iprime = 2 * j + hh
                qoff = (hh * 2) * S
                koff = (hh * 2 + 1) * S
                for qc in range(4):
                    eb = qc % 2
                    exp_base = C_["exp"].n
                    for kg in range(8):
                        for k2 in range(2):
                            kt = kg * 2 + k2
                            bank = 4 + (kg % 2) * 2 + k2
                            if k2 == 0 and exp_base + kg - 1 > 0:
                                te.wait_ge(C_["exp"].sem, exp_base + kg - 1)
                            m2c = C_["m2"].inc(te.matmul(
                                pb[bank][:, :],
                                s_qkT[:, koff + kt * 128: koff + (kt + 1) * 128],
                                s_qkT[:, qoff + qc * 512: qoff + (qc + 1) * 512],
                                start=True, stop=True))
                        sc.wait_ge(C_["m2"].sem, m2c)
                        if kg == 0 and expT_reader[eb] > 0:
                            sc.wait_ge(C_["rtt"].sem, dacc_mm[eb])
                        C_["exp"].inc(sc.activation(
                            s_expT[eb][:, kg * 1024:(kg + 1) * 1024],
                            pbB[:, (kg % 2) * 1024:(kg % 2) * 1024 + 1024],
                            ACT.Exp, bias=EXP_SHIFT, scale=INV_SQRT_HD))
                        if pend_m3 is not None:
                            emit_prev_chunk(pend_m3, kg)
                    if pend_m3 is not None:
                        finish_prev(pend_m3)
                    pend_m3 = (eb, hh, iprime, qc, C_["exp"].n)

            if pend_m3 is not None:
                for kg in range(8):
                    emit_prev_chunk(pend_m3, kg)
                finish_prev(pend_m3)
                pend_m3 = None
            if pend is not None:
                emit_bcast_ctx(pend)
                pend = None

        if debug:
            gp.wait_ge(C_["ctx"].sem, C_["ctx"].n)
            C_["dbgg"].inc(gp.dma_start(out=dbg["ctxT"][:, :], in_=s_ctxT[:]), 16)
            gp.wait_ge(C_["dbgg"].sem, C_["dbgg"].n)

        drip_casts(len(cast_jobs))
        nc.all_engine_barrier()
        p1ctx.close()
        if stop_after == "P1":
            _finish(nc, sy, gp, C_)
            return nc, dbg

        p2ctx = ctx.enter_context(ExitStack())
        E2 = p2ctx.enter_context
        s_wprojT = E2(nc.sbuf_tensor("s_wprojT", [128, NHT * H], BF16))
        s_bprojr = E2(nc.sbuf_tensor("s_bprojr", [1, H], F32))
        C_["ld0"].inc(sy.dma_start(out=s_bprojr[:], in_=bprojr_d[:, :]), 16)
        bprojr_ld = C_["ld0"].n
        s_xres = _ring2(E2, nc, "s_xres", H, F32)
        s_xtc = _ring2(E2, nc, "s_xtc", H, F32)
        s_t = _ring2(E2, nc, "s_t", H, F32)
        s_x1t = _ring2(E2, nc, "s_x1t", H, BF16)
        s_rt = E2(nc.sbuf_tensor("s_rt", [128, 128], F32))
        s_gn = E2(nc.sbuf_tensor("s_gn", [128, 128], F32))
        s_p8w = E2(nc.sbuf_tensor("s_p8w", [128, 128], F32))
        s_selw = E2(nc.sbuf_tensor("s_selw", [128, 128], F32))
        s_eqw = E2(nc.sbuf_tensor("s_eqw", [128, 128], mybir.dt.uint8))
        s_mx16 = E2(nc.sbuf_tensor("s_mx16", [128, 16], F32))
        s_Z16 = E2(nc.sbuf_tensor("s_Z16", [128, 16], F32))
        s_lnZ = E2(nc.sbuf_tensor("s_lnZ", [128, 16], F32))
        s_gsub = E2(nc.sbuf_tensor("s_gsub", [128, 16], F32))
        s_io8w = E2(nc.sbuf_tensor("s_io8w", [128, 128], F32))
        s_big8w = E2(nc.sbuf_tensor("s_big8w", [128, 128], F32))
        C_["rcp"].inc(sy.dma_start(out=s_io8w[:], in_=iota8w_d.ap()), 16)
        C_["rcp"].inc(sy.dma_start(out=s_big8w[:], in_=big8w_d.ap()), 16)
        io8w_ld = C_["rcp"].n

        for ht in range(NHT):
            C_["wp"].inc(gp.dma_start(
                out=s_wprojT[:, ht * H:(ht + 1) * H], in_=wprojT_d[ht]), 16)

        xres_at, xtc_at = {}, {}

        def p2_loads(st):
            r = st % 2
            if st >= 2:
                sy.wait_ge(C_["tt"].sem, st - 1)
            ch = C_["xres%d" % r]
            xres_at[st] = (ch, ch.inc(sy.dma_start(
                out=s_xres[r][:, :], in_=x_nat[st * 128:(st + 1) * 128, :]), 16))
            if st >= 2:
                sy.wait_ge(C_["m4r"].sem, st - 1)
            ch2 = C_["xtc%d" % r]
            xtc_at[st] = (ch2, ch2.inc(sy.dma_start(
                out=s_xtc[r][:, :], in_=xTc[st]), 16))

        p2_loads(0)
        p2_loads(1)
        for st in range(NST):
            r = st % 2
            if st + 2 < NST:
                p2_loads(st + 2)
            if st >= 1:
                te.wait_ge(C_["tt"].sem, st)
            if st == 0:
                te.wait_ge(C_["wp"].sem, 16 * NHT)
            for ht in range(NHT):
                l_ap = s_ctxT[:, ht * S + st * 128: ht * S + st * 128 + 128]
                for hc in range(4):
                    te.matmul(pb[hc][:, :], l_ap,
                              s_wprojT[:, ht * H + hc * 512: ht * H + (hc + 1) * 512],
                              start=(ht == 0), stop=False)
            if st == 0:
                te.wait_ge(C_["ld0"].sem, bprojr_ld)
            for hc in range(4):
                mm_i = te.matmul(pb[hc][:, :], s_1rf[0:1, :],
                                 s_bprojr[0:1, hc * 512:(hc + 1) * 512],
                                 start=False, stop=True)
            m4c = C_["m4"].inc(mm_i)
            if st >= 1:
                te.wait_ge(C_["rt8"].sem, st)
            te.wait_ge(xtc_at[st][0].sem, xtc_at[st][1])
            for ht in range(NHT):
                te.matmul(pb[6][:, 0:8],
                          s_ctxT[:, ht * S + st * 128: ht * S + st * 128 + 128],
                          s_wpb[:, ht * 8:(ht + 1) * 8],
                          start=(ht == 0), stop=False)
            for ht in range(NHT):
                mm_i = te.matmul(pb[6][:, 0:8],
                                 s_xtc[r][:, ht * 128:(ht + 1) * 128],
                                 s_wg[:, ht * 8:(ht + 1) * 8],
                                 start=False, stop=(ht == NHT - 1))
            m4rc = C_["m4r"].inc(mm_i)

            ve.wait_ge(C_["m4"].sem, m4c)
            ve.wait_ge(xres_at[st][0].sem, xres_at[st][1])
            ve.tensor_tensor(s_t[r][:, :], pbA[:, :], s_xres[r][:, :], A.add)
            ve.drain()
            C_["tt"].inc(ve.nop())
            for hc in range(4):
                ve.bn_stats(s_bn[:, hc * 6:(hc + 1) * 6],
                            s_t[r][:, hc * 512:(hc + 1) * 512])
            ve.drain()
            ve.bn_aggr(s_mv[:], s_bn[:])
            ve.drain()
            lnc = C_["lnv"].inc(ve.nop())
            sc.wait_ge(C_["lnv"].sem, lnc)
            sc.activation(s_sig[:], s_mv[:, 1:2], ACT.Ln, bias=EPS)
            sc.drain()
            sqc = C_["sqc"].inc(sc.activation(s_rsig16[:, st:st + 1], s_sig[:],
                                              ACT.Exp, scale=-0.5))
            ve.wait_ge(C_["sqc"].sem, sqc)
            ve.tensor_scalar(s_nmr16[:, st:st + 1], s_mv[:, 0:1],
                             s_rsig16[:, st:st + 1], -1.0,
                             op0=A.mult, op1=A.mult)
            ve.drain()
            nmc = C_["rtv"].inc(ve.nop())
            sc.wait_ge(C_["rtv"].sem, nmc)
            if st >= 2:
                sc.wait_ge(C_["x1w%d" % r].sem, 16 * (st // 2))
            sc.activation(s_x1t[r][:, :], s_t[r][:, :], ACT.Identity,
                          bias=s_nmr16[:, st:st + 1], scale=s_rsig16[:, st:st + 1])
            sc.drain()
            x1vc = C_["x1v"].inc(sc.nop())
            sy.wait_ge(C_["x1v"].sem, x1vc)
            C_["x1w%d" % r].inc(sy.dma_start(
                out=x1_stage[st * 128:(st + 1) * 128, :], in_=s_x1t[r][:, :]), 16)

            sc.wait_ge(C_["m4r"].sem, m4rc)
            C_["rt8"].inc(sc.copy(s_rt[:, st * 8:(st + 1) * 8], pb[6][:, 0:8]))

        rt3 = s_rt.ap().rearrange("p (s e) -> p s e", s=16)
        p83 = s_p8w.ap().rearrange("p (s e) -> p s e", s=16)
        sel3 = s_selw.ap().rearrange("p (s e) -> p s e", s=16)
        bwgB = s_bwg.ap().unsqueeze(1).broadcast_to([128, 16, 8])
        GB = s_G.ap().unsqueeze(1).broadcast_to([128, 16, 8])
        rsB = s_rsig16.ap().unsqueeze(2).broadcast_to([128, 16, 8])
        nmB = s_nmr16.ap().unsqueeze(2).broadcast_to([128, 16, 8])
        mxB = s_mx16.ap().unsqueeze(2).broadcast_to([128, 16, 8])
        ve.wait_ge(C_["rt8"].sem, C_["rt8"].n)
        ve.tensor_tensor(rt3, rt3, bwgB, A.add)
        ve.drain()
        ve.tensor_tensor(rt3, rt3, rsB, A.mult)
        ve.tensor_tensor(s_gn.ap().rearrange("p (s e) -> p s e", s=16),
                         GB, nmB, A.mult)
        ve.drain()
        ve.tensor_tensor(s_rt[:], s_rt[:], s_gn[:], A.add)
        ve.drain()
        ve.tensor_reduce(s_mx16.ap().unsqueeze(2), rt3, axis=AX, op=A.max)
        ve.drain()
        ve.tensor_tensor(s_eqw.ap().rearrange("p (s e) -> p s e", s=16),
                         rt3, mxB, A.is_ge)
        ve.drain()
        ve.wait_ge(C_["rcp"].sem, io8w_ld)
        ve.select(s_selw[:], s_eqw[:], s_io8w[:], s_big8w[:], add_drain=True)
        ve.drain()
        ve.tensor_reduce(s_eidxb.ap().unsqueeze(2), sel3, axis=AX, op=A.min)
        ve.drain()
        bvc = C_["rtv"].inc(ve.nop())
        sc.wait_ge(C_["rtv"].sem, bvc)
        sc.activation(s_p8w[:], s_rt[:], ACT.Exp)
        sc.drain()
        pvc = C_["sqc"].inc(sc.nop())
        ve.wait_ge(C_["sqc"].sem, pvc)
        ve.tensor_reduce(s_Z16.ap().unsqueeze(2), p83, axis=AX, op=A.add)
        ve.drain()
        zvc = C_["rtv"].inc(ve.nop())
        sc.wait_ge(C_["rtv"].sem, zvc)
        sc.activation(s_lnZ[:], s_Z16[:], ACT.Ln)
        sc.drain()
        lzc = C_["sqc"].inc(sc.nop())
        ve.wait_ge(C_["sqc"].sem, lzc)
        ve.tensor_tensor(s_gsub[:], s_mx16[:], s_lnZ[:], A.subtract)
        ve.drain()
        gvc = C_["rtv"].inc(ve.nop())
        sc.wait_ge(C_["rtv"].sem, gvc)
        C_["gexp"].inc(sc.activation(s_gate[:], s_gsub[:], ACT.Exp))

        if debug:
            gp.wait_ge(C_["x1w0"].sem, C_["x1w0"].n)
            gp.wait_ge(C_["x1w1"].sem, C_["x1w1"].n)
            C_["dbgg"].inc(gp.dma_start(out=dbg["x1"][:, :], in_=x1_stage[:, :]), 16)
            sy.wait_ge(C_["gexp"].sem, C_["gexp"].n)
            C_["dbgw"].inc(sy.dma_start(out=dbg["gate"][:, :], in_=s_gate[:]), 16)
            gp.wait_ge(C_["dbgg"].sem, C_["dbgg"].n)
            sy.wait_ge(C_["dbgw"].sem, C_["dbgw"].n)

        nc.all_engine_barrier()
        p2ctx.close()
        p12ctx.close()
        if stop_after == "P2":
            _finish(nc, sy, gp, C_)
            return nc, dbg

        NQ, QW = 4, 512
        mwctx = ctx.enter_context(ExitStack())
        EW = mwctx.enter_context
        s_wbig = EW(nc.sbuf_tensor("s_wbig", [128, NHHT * H], BF16))
        s_w13 = _ring2(EW, nc, "s_w13", 2 * H, BF16)
        w13chs = [C_["w13a"], C_["w13b"]]
        w2ch = C_["w2l"]

        def preload_ffn(w1src, w3src, w2src):
            pre = {}
            sy.wait_ge(C_["wcast"].sem, wcast_final)
            sc.wait_ge(C_["wcast"].sem, wcast_final)
            for hhb in (0, 1):
                eng = sy if hhb % 2 == 0 else sc
                ch = w13chs[hhb % 2]
                ch.inc(eng.dma_start(out=s_w13[hhb % 2][:, 0:H], in_=w1src[hhb]), 16)
                ch.inc(eng.dma_start(out=s_w13[hhb % 2][:, H:2 * H], in_=w3src[hhb]), 16)
                pre[hhb] = (ch, ch.n)
            for hhb in range(NHHT):
                w2ch.inc(sy.dma_start(
                    out=s_wbig[:, hhb * H:(hhb + 1) * H], in_=w2src[hhb]), 16)
            pre["w2_final"] = w2ch.n
            return pre

        p3ctx = ctx.enter_context(ExitStack())
        E3 = p3ctx.enter_context
        s_eT = E3(nc.sbuf_tensor("s_eT", [16, 128], F32))
        s_eG = E3(nc.sbuf_tensor("s_eG", [128, 128], F32))
        s_cmp8 = E3(nc.sbuf_tensor("s_cmp8", [128, 8 * 128], F32))
        s_scan8 = E3(nc.sbuf_tensor("s_scan8", [128, 8 * 128], F32))
        s_tot8 = E3(nc.sbuf_tensor("s_tot8", [128, 8], F32))
        s_off8 = E3(nc.sbuf_tensor("s_off8", [128, 8], F32))
        s_zero = E3(nc.sbuf_tensor("s_zero", [128, 128], F32))
        s_neg1 = E3(nc.sbuf_tensor("s_neg1", [128, 128], F32))
        s_pos = E3(nc.sbuf_tensor("s_pos", [128, 128], F32))
        s_ptmp = E3(nc.sbuf_tensor("s_ptmp", [128, 128], F32))
        s_keep = E3(nc.sbuf_tensor("s_keep", [128, 128], F32))
        s_flat = E3(nc.sbuf_tensor("s_flat", [128, 128], F32))
        s_cmpmy = E3(nc.sbuf_tensor("s_cmpmy", [128, 128], F32))
        s_iotaB = E3(nc.sbuf_tensor("s_iotaB", [128, 2048], F32))
        s_iotaBi = E3(nc.sbuf_tensor("s_iotaBi", [128, 2048], mybir.dt.int32))
        s_iotaC = E3(nc.sbuf_tensor("s_iotaC", [128, 16], F32))
        s_iotaCi = E3(nc.sbuf_tensor("s_iotaCi", [128, 16], mybir.dt.int32))
        s_scanMy = E3(nc.sbuf_tensor("s_scanMy", [128, 128], F32))
        s_totMy = E3(nc.sbuf_tensor("s_totMy", [128, 1], F32))
        s_offMy = E3(nc.sbuf_tensor("s_offMy", [128, 1], F32))
        s_doff = E3(nc.sbuf_tensor("s_doff", [128, 1], F32))
        s_flocal = E3(nc.sbuf_tensor("s_flocal", [128, 128], F32))
        s_cmpJ = E3(nc.sbuf_tensor("s_cmpJ", [128, 128], F32))
        s_pA = E3(nc.sbuf_tensor("s_pA", [128, 16], F32))
        s_pB = E3(nc.sbuf_tensor("s_pB", [128, 16], F32))
        s_kof = E3(nc.sbuf_tensor("s_kof", [128, 16], F32))
        s_pou = E3(nc.sbuf_tensor("s_pou", [128, 16], U32))
        s_fl = E3(nc.sbuf_tensor("s_fl", [128, 128], F32))
        s_oneh = E3(nc.sbuf_tensor("s_oneh", [128, 128], F32))
        s_fof = E3(nc.sbuf_tensor("s_fof", [128, 1], F32))
        s_shift1 = E3(nc.sbuf_tensor("s_shift1", [128, 128], F32))
        s_ko = E3(nc.sbuf_tensor("s_ko", [16, 128], F32))
        s_fo = E3(nc.sbuf_tensor("s_fo", [16, 128], F32))
        s_keepT = E3(nc.sbuf_tensor("s_keepT", [128, 16], F32))
        s_stokF = E3(nc.sbuf_tensor("s_stokF", [128, 16], F32))
        s_retF = E3(nc.sbuf_tensor("s_retF", [128, 16], F32))

        eitc = C_["rtt"].inc(te.transpose(pb[0][0:16, 0:128], s_eidxb[:], s_idf[:]))
        sc.wait_ge(C_["rtt"].sem, eitc)
        eisc = C_["rte"].inc(sc.copy(s_eT[:], pb[0][0:16, 0:128]))
        gp.wait_ge(C_["rte"].sem, eisc)
        C_["eix"].inc(gp.dma_start(out=eidx_stage[:, :], in_=s_eT[:]), 16)
        gp.wait_ge(C_["eix"].sem, 16)
        C_["age"].inc(gp.collective_compute(
            "AllGather", A.bypass, replica_groups=[list(range(8))],
            ins=[eidx_stage.ap().opt()], outs=[eidx_all.ap().opt()]))
        gp.wait_ge(C_["x1w0"].sem, C_["x1w0"].n)
        gp.wait_ge(C_["x1w1"].sem, C_["x1w1"].n)
        C_["agx"].inc(gp.collective_compute(
            "AllGather", A.bypass, replica_groups=[list(range(8))],
            ins=[x1_stage.ap().opt()], outs=[x1_all.ap().opt()]))
        pre_exp = preload_ffn(w1b, w3b, w2b)
        gp.wait_ge(C_["age"].sem, 1)
        eirc = C_["eir"].inc(gp.dma_start(out=s_eG[:], in_=eidx_all[:, :]), 16)
        C_["ld0"].inc(sy.dma_start(out=s_shift1[:], in_=shift1_d.ap()), 16)
        C_["xt"].inc(gp.iota(s_iotaBi[:], [[1, 2048]], base=0,
                             channel_multiplier=0,
                             allow_small_or_imprecise_dtypes=True))
        C_["xt"].inc(gp.iota(s_iotaCi[:], [[128, 16]], base=0,
                             channel_multiplier=1,
                             allow_small_or_imprecise_dtypes=True))
        shift1_ld = C_["ld0"].n

        ve.memset(s_zero[:], 0.0)
        ve.memset(s_neg1[:], -1.0)
        ve.memset(s_pos[:], 0.0)
        ve.drain()
        ve.wait_ge(C_["xt"].sem, C_["xt"].n)
        ve.tensor_copy(s_iotaB[:], s_iotaBi[:])
        ve.tensor_copy(s_iotaC[:], s_iotaCi[:])
        ve.drain()
        ve.wait_ge(C_["eir"].sem, eirc)
        for e in range(8):
            ve.tensor_scalar(s_cmp8[:, e * 128:(e + 1) * 128], s_eG[:], float(e),
                             None, op0=A.is_equal)
        ve.drain()
        for e in range(8):
            ve.tensor_tensor_scan(s_scan8[:, e * 128:(e + 1) * 128],
                                  s_cmp8[:, e * 128:(e + 1) * 128], s_zero[:],
                                  initial=0.0, op0=A.add, op1=A.add)
        ve.drain()
        for e in range(8):
            ve.tensor_copy(s_tot8[:, e:e + 1],
                           s_scan8[:, (e + 1) * 128 - 1:(e + 1) * 128])
        ve.drain()
        rv1 = C_["rtv"].inc(ve.nop())
        te.wait_ge(C_["rtv"].sem, rv1)
        rtt1 = C_["rtt"].inc(te.matmul(pb[1][:, 0:8], s_tri[:], s_tot8[:],
                                       start=True, stop=True))
        ve.wait_ge(C_["rtt"].sem, rtt1)
        ve.tensor_copy(s_off8[:], pb[1][:, 0:8])
        ve.drain()
        for e in range(8):
            ve.tensor_scalar(s_ptmp[:], s_scan8[:, e * 128:(e + 1) * 128],
                             s_off8[:, e:e + 1], -1.0, op0=A.add, op1=A.add)
            ve.drain()
            ve.tensor_tensor(s_ptmp[:], s_ptmp[:],
                             s_cmp8[:, e * 128:(e + 1) * 128], A.mult)
            ve.drain()
            ve.tensor_tensor(s_pos[:], s_pos[:], s_ptmp[:], A.add)
            ve.drain()
        ve.tensor_scalar(s_keep[:], s_pos[:], float(CAP), None, op0=A.is_lt)
        ve.tensor_scalar(s_flat[:], s_eG[:], float(CAP), None, op0=A.mult)
        ve.drain()
        ve.tensor_tensor(s_flat[:], s_flat[:], s_pos[:], A.add)
        ve.drain()
        ve.tensor_scalar(s_flat[:], s_flat[:], 16383.0, None, op0=A.min)
        ve.tensor_scalar(s_cmpmy[:], s_eG[:], s_mye[:], None, op0=A.is_equal)
        ve.drain()
        ve.tensor_tensor_scan(s_scanMy[:], s_cmpmy[:], s_zero[:],
                              initial=0.0, op0=A.add, op1=A.add)
        ve.drain()
        ve.tensor_copy(s_totMy[:], s_scanMy[:, 127:128])
        ve.drain()
        rv2 = C_["rtv"].inc(ve.nop())
        te.wait_ge(C_["rtv"].sem, rv2)
        te.wait_ge(C_["ld0"].sem, shift1_ld)
        te.matmul(pb[2][:, 0:1], s_tri[:], s_totMy[:], start=True, stop=True)
        rtt2a = C_["rtt"].inc(te.matmul(pb[2][:, 1:2], s_shift1[:], s_totMy[:],
                                        start=True, stop=True))
        ve.wait_ge(C_["rtt"].sem, rtt2a)
        ve.tensor_copy(s_offMy[:], pb[2][:, 0:1])
        ve.tensor_copy(s_doff[:], pb[2][:, 1:2])
        ve.drain()
        ve.memset(s_flocal[:], 0.0)
        ve.drain()
        for f in range(128):
            ve.scalar_tensor_tensor(s_flocal[:], s_iotaB[:, 0:128],
                                    s_scanMy[:, f:f + 1], s_flocal[:],
                                    op0=A.is_ge, op1=A.add)
            ve.drain()
        rv3 = C_["rtv"].inc(ve.nop())
        gp.wait_ge(C_["rtv"].sem, rv3)
        flw = C_["stok"].inc(gp.dma_start(out=floc_hbm[:, :], in_=s_flocal[:]), 16)
        te.wait_ge(C_["rtv"].sem, rv3)
        for J in range(16):
            ve.tensor_scalar(s_cmpJ[:], s_iotaB[:, J * 128:(J + 1) * 128],
                             s_offMy[:], None, op0=A.is_ge)
            ve.drain()
            rvJ = C_["rtv"].inc(ve.nop())
            te.wait_ge(C_["rtv"].sem, rvJ)
            C_["rtt"].inc(te.matmul(pb[3][:, J:J + 1], s_cmpJ[:], s_1cf[:],
                                    start=True, stop=True))
            mmJ = C_["rtt"].inc(te.matmul(pb[4][:, J:J + 1], s_cmpJ[:], s_doff[:],
                                          start=True, stop=True))
            ve.wait_ge(C_["rtt"].sem, mmJ)
        ve.tensor_copy(s_pA[:], pb[3][:, 0:16])
        ve.tensor_copy(s_pB[:], pb[4][:, 0:16])
        ve.drain()
        ve.tensor_scalar(s_pA[:], s_pA[:], -1.0, None, op0=A.add)
        ve.tensor_tensor(s_kof[:], s_iotaC[:], s_pB[:], A.subtract)
        ve.drain()
        ve.tensor_copy(s_pou[:], s_pA[:])
        ve.drain()
        rv3b = C_["rtv"].inc(ve.nop())
        gp.wait_ge(C_["stok"].sem, flw)
        gp.wait_ge(C_["rtv"].sem, rv3b)
        for J in range(16):
            gxc = C_["gx"].inc(gp.indirect_dma_start(
                out=s_fl[:], out_offset=None, in_=floc_hbm[:, :],
                in_offset=IndirectOffsetOnAxis(s_pou[:, J:J + 1], axis=0)), 16)
            ve.wait_ge(C_["gx"].sem, gxc)
            ve.tensor_scalar(s_oneh[:], s_iotaB[:, 0:128], s_kof[:, J:J + 1],
                             None, op0=A.is_equal)
            ve.drain()
            ve.tensor_tensor(s_oneh[:], s_oneh[:], s_fl[:], A.mult)
            ve.drain()
            ve.reduce_sum(s_fof[:], s_oneh[:], axis=AX)
            ve.drain()
            ve.tensor_scalar(s_stokF[:, J:J + 1], s_pA[:, J:J + 1], 128.0,
                             s_fof[:, 0:1], op0=A.mult, op1=A.add)
            ve.drain()
            ve.tensor_scalar(s_stokF[:, J:J + 1], s_stokF[:, J:J + 1], 16383.0,
                             None, op0=A.min)
            ve.drain()
            gvJ = C_["rtv"].inc(ve.nop())
            gp.wait_ge(C_["rtv"].sem, gvJ)
        ve.tensor_copy(s_stokU[:], s_stokF[:])
        ve.drain()
        rv3c = C_["rtv"].inc(ve.nop())
        te.wait_ge(C_["rtv"].sem, rv3c)
        te.matmul(pb[3][0:16, 0:128], s_sel[:], s_keep[:], start=True, stop=True)
        rtt2 = C_["rtt"].inc(te.matmul(pb[4][0:16, 0:128], s_sel[:], s_flat[:],
                                       start=True, stop=True))
        sc.wait_ge(C_["rtt"].sem, rtt2)
        sc.copy(s_ko[:], pb[3][0:16, 0:128])
        rte2 = C_["rte"].inc(sc.copy(s_fo[:], pb[4][0:16, 0:128]))
        te.wait_ge(C_["rte"].sem, rte2)
        te.transpose(pb[5][:, 0:16], s_ko[:], s_idf[0:16, 0:16])
        rtt3 = C_["rtt"].inc(te.transpose(pb[6][:, 0:16], s_fo[:],
                                          s_idf[0:16, 0:16]))
        ve.wait_ge(C_["rtt"].sem, rtt3)
        ve.tensor_copy(s_keepT[:], pb[5][:, 0:16])
        ve.tensor_copy(s_retF[:], pb[6][:, 0:16])
        ve.drain()
        ve.wait_ge(C_["gexp"].sem, C_["gexp"].n)
        ve.tensor_tensor(s_gk[:], s_gate[:], s_keepT[:], A.mult)
        ve.drain()
        ve.tensor_copy(s_retiU[:], s_retF[:])
        ve.drain()
        rv4 = C_["rtv"].inc(ve.nop())

        if debug:
            sy.wait_ge(C_["rtv"].sem, rv4)
            C_["dbgw"].inc(sy.dma_start(out=dbg["eidx"][:, :], in_=eidx_stage[:, :]), 16)
            C_["dbgw"].inc(sy.dma_start(out=dbg["pos"][:, :], in_=s_pos[:]), 16)
            C_["dbgw"].inc(sy.dma_start(out=dbg["keep"][:, :], in_=s_keep[:]), 16)
            C_["dbgw"].inc(sy.dma_start(out=dbg["stok"][:, :], in_=s_stokF[:]), 16)
            C_["dbgw"].inc(sy.dma_start(out=dbg["gk"][:, :], in_=s_gk[:]), 16)
            C_["dbgw"].inc(sy.dma_start(out=dbg["reti"][:, :], in_=s_retF[:]), 16)
            C_["dbgw"].inc(sy.dma_start(out=dbg["eg"][:, :], in_=s_eG[:]), 16)
            sy.wait_ge(C_["dbgw"].sem, C_["dbgw"].n)

        nc.all_engine_barrier()
        p3ctx.close()
        if stop_after == "P3":
            _finish(nc, sy, gp, C_)
            return nc, dbg

        p4ctx = ctx.enter_context(ExitStack())
        E4 = p4ctx.enter_context
        arena = E4(nc.sbuf_tensor("arena", [128, 2 * NHT * QW + 2 * 2048], BF16))
        s_h1 = E4(nc.sbuf_tensor("s_h1", [128, NHHT * QW], BF16))
        s_silu = E4(nc.sbuf_tensor("s_silu", [128, 1024], F32))
        s_oe = _ring2(E4, nc, "s_oe", H, BF16)
        s_x1rd = _ring2(E4, nc, "s_x1rd", H, BF16)
        s_oesr = s_oe
        s_moe = E4(nc.sbuf_tensor("s_moe", [128, H], BF16))
        s_x1f = E4(nc.sbuf_tensor("s_x1f", [128, H], BF16))
        s_t2 = E4(nc.sbuf_tensor("s_t2", [128, H], F32))
        Rh = [arena[:, 0:NHT * QW], arena[:, NHT * QW:2 * NHT * QW]]
        s_xdt = [arena[:, 2 * NHT * QW:2 * NHT * QW + 2048],
                 arena[:, 2 * NHT * QW + 2048:2 * NHT * QW + 2 * 2048]]

        last_tpe = [None]

        def transpose_tiles_to_R(qq, k, src):
            for grpi, bank in ((0, 6), (1, 7)):
                if C_["tpo"].n >= 2:
                    te.wait_ge(C_["tpe"].sem, C_["tpo"].n - 1)
                for h2 in range(8):
                    ht = grpi * 8 + h2
                    mm_i = te.transpose(pbb[bank][:, h2 * 128:(h2 + 1) * 128],
                                        src[:, ht * 128:(ht + 1) * 128], s_idb[:])
                tpc = C_["tpo"].inc(mm_i)
                sc.wait_ge(C_["tpo"].sem, tpc)
                for h2 in range(8):
                    ht = grpi * 8 + h2
                    sc_i = sc.copy(Rh[qq % 2][:, ht * QW + k * 128: ht * QW + (k + 1) * 128],
                                   pbb[bank][:, h2 * 128:(h2 + 1) * 128])
                last_tpe[0] = sc_i
                C_["tpe"].inc(sc_i)

        x1q_steps = [0]
        xdt_steps = [0]

        def build_x1q(qq, k):
            st = qq * 4 + k
            rr = k % 2
            if x1q_steps[0] >= 2:
                gp.wait_ge(C_["tpe"].sem, C_["tpe"].n - 2)
            x1q_steps[0] += 1
            ch = C_["x1rA" if rr == 0 else "x1rB"]
            x1rc = ch.inc(gp.dma_start(
                out=s_x1rd[rr][:, :],
                in_=x1_stage[st * 128:(st + 1) * 128, :]), 16)
            te.wait_ge(ch.sem, x1rc)
            transpose_tiles_to_R(qq, k, s_x1rd[rr])
            return C_["tpe"].n

        def gather_q(qq, k):
            jj = qq * 4 + k
            rr = k % 2
            gp.wait_ge(C_["agx"].sem, 1)
            if xdt_steps[0] >= 2:
                gp.wait_ge(C_["tpe"].sem, C_["tpe"].n - 2)
            xdt_steps[0] += 1
            ch = C_["gx" if rr == 0 else "gx2"]
            gxc = ch.inc(gp.indirect_dma_start(
                out=s_xdt[rr], out_offset=None, in_=x1_all[:, :],
                in_offset=IndirectOffsetOnAxis(s_stokU[:, jj:jj + 1], axis=0)), 16)
            te.wait_ge(ch.sem, gxc)
            transpose_tiles_to_R(qq, k, s_xdt[rr])
            return C_["tpe"].n

        ret_at, x1f_at = {}, {}
        oes_ready = {}
        dbg_base = [0]

        def p6_loads(st):
            if st == 0:
                gp.wait_ge(C_["agoe"].sem, 1)
            if st >= 1:
                gp.wait_ge(C_["yv"].sem, st)
            ch = C_["retA" if st % 2 == 0 else "retB"]
            ret_at[st] = (ch, ch.inc(gp.indirect_dma_start(
                out=s_moe[:, :], out_offset=None, in_=oe_all[:, :],
                in_offset=IndirectOffsetOnAxis(s_retiU[:, st:st + 1], axis=0)), 16))
            if st >= 1:
                gp.wait_ge(C_["yv"].sem, st)
            ch = C_["x1fA" if st % 2 == 0 else "x1fB"]
            x1f_at[st] = (ch, ch.inc(gp.dma_start(
                out=s_x1f[:, :], in_=x1_stage[st * 128:(st + 1) * 128, :]), 16))

        def p6_stile(st):
            rr = st % 2
            ve.wait_ge(ret_at[st][0].sem, ret_at[st][1])
            ve.wait_ge(x1f_at[st][0].sem, x1f_at[st][1])
            ve.wait_ge(C_["oev"].sem, oes_ready[st])
            if st >= 1:
                pch = "ywA" if (st - 1) % 2 == 0 else "ywB"
                ve.wait_ge(C_[pch].sem, 16 * ((st - 1) // 2 + 1))
            ve.scalar_tensor_tensor(s_t2[:, :], s_moe[:, :],
                                    s_gk[:, st:st + 1], s_x1f[:, :],
                                    op0=A.mult, op1=A.add)
            ve.drain()
            ve.tensor_tensor(s_t2[:, :], s_t2[:, :], s_oesr[rr][:, :], A.add)
            ve.drain()
            for hc in range(4):
                ve.bn_stats(s_bn[:, hc * 6:(hc + 1) * 6],
                            s_t2[:, hc * 512:(hc + 1) * 512])
            ve.drain()
            ve.bn_aggr(s_mv[:], s_bn[:])
            ve.drain()
            lnc = C_["lnv"].inc(ve.nop())
            sc.wait_ge(C_["lnv"].sem, lnc)
            sc.activation(s_sig[:], s_mv[:, 1:2], ACT.Ln, bias=EPS)
            sc.drain()
            sqc = C_["sqc"].inc(sc.activation(s_rsig[:], s_sig[:], ACT.Exp, scale=-0.5))
            ve.wait_ge(C_["sqc"].sem, sqc)
            ve.tensor_scalar(s_nmr[:], s_mv[:, 0:1], s_rsig[:], -1.0,
                             op0=A.mult, op1=A.mult)
            ve.drain()
            ve.tensor_scalar(s_t2[:, :], s_t2[:, :], s_rsig[:], s_nmr[:],
                             op0=A.mult, op1=A.add)
            ve.drain()
            yvc = C_["yv"].inc(ve.nop())
            sy.wait_ge(C_["yv"].sem, yvc)
            C_["ywA" if rr == 0 else "ywB"].inc(sy.dma_start(
                out=y[st * 128:(st + 1) * 128, :], in_=s_t2[:, :]), 16)

        def post_tile(st):
            p6_loads(st)
            p6_stile(st)
            if debug:
                gp.wait_ge(C_["oev"].sem, oes_ready[st])
                gp.wait_ge(C_["dbgg"].sem, C_["dbgg"].n)
                C_["dbgg"].inc(gp.dma_start(
                    out=dbg["oes"][st * 128:(st + 1) * 128, :],
                    in_=s_oesr[st % 2][:, :]), 16)

        h1_free = [0, 0]
        w13_free = [0, 0]
        oe_bank_free = [0, 0]
        oe_ring_free = [0, 0]

        def ffn_phase(w1src, w3src, w2src, out_stage, ready, build_next,
                      evac_hbm=True, post_q=None, tail_build=None,
                      w2_wait_m7=False, pre=None, last_cast=False):
            if pre:
                w2_target = pre["w2_final"]
                w2s_target = C_["w2s"].n
            else:
                w2_target = w2ch.n + 16 * ((NHHT + 1) // 2)
                w2s_target = C_["w2s"].n + 16 * (NHHT // 2)
            m7_entry = C_["m7"].n
            if last_cast:
                sy.wait_ge(C_["wcast"].sem, wcast_final)
                sc.wait_ge(C_["wcast"].sem, wcast_final)
            oew_final = [C_["oewA"].n, C_["oewB"].n]
            if not evac_hbm:
                dbg_base[0] = C_["dbgg"].n
            tail_ready = [None]
            for qq in range(NQ):
                w13_at = {}

                def load_w13(hhb, qq=qq):
                    rr2 = hhb % 2
                    eng = sy if rr2 == 0 else sc
                    if w13_free[rr2] > 0:
                        eng.wait_ge(C_["m56"].sem, w13_free[rr2])
                    w13ch = w13chs[rr2]
                    w13ch.inc(eng.dma_start(out=s_w13[rr2][:, 0:H], in_=w1src[hhb]), 16)
                    w13ch.inc(eng.dma_start(out=s_w13[rr2][:, H:2 * H], in_=w3src[hhb]), 16)
                    w13_at[hhb] = (w13ch, w13ch.n)
                    if qq == 0 and pre is None:
                        if hhb == 0 and w2_wait_m7:
                            sy.wait_ge(C_["m7"].sem, m7_entry)
                        if hhb % 2 == 0:
                            w2ch.inc(sy.dma_start(
                                out=s_wbig[:, hhb * H:(hhb + 1) * H],
                                in_=w2src[hhb]), 16)
                        else:
                            C_["w2s"].inc(sc.dma_start(
                                out=s_wbig[:, hhb * H:(hhb + 1) * H],
                                in_=w2src[hhb]), 16)

                if qq == 0 and pre is not None:
                    w13_at[0] = pre[0]
                    w13_at[1] = pre[1]
                else:
                    load_w13(0)
                    load_w13(1)
                te.wait_ge(C_["tpe"].sem, ready[qq])
                for hhb in range(NHHT):
                    rr2 = hhb % 2
                    par = hhb % 2
                    te.wait_ge(w13_at[hhb][0].sem, w13_at[hhb][1])
                    if h1_free[par] > 0:
                        te.wait_ge(C_["h1"].sem, h1_free[par])
                    bk = par * 2
                    for ht in range(NHT):
                        te.matmul(pb[bk][:, :],
                                  s_w13[rr2][:, ht * 128:(ht + 1) * 128],
                                  Rh[qq % 2][:, ht * QW:(ht + 1) * QW],
                                  start=(ht == 0), stop=(ht == NHT - 1))
                    for ht in range(NHT):
                        mm_i = te.matmul(pb[bk + 1][:, :],
                                         s_w13[rr2][:, H + ht * 128: H + (ht + 1) * 128],
                                         Rh[qq % 2][:, ht * QW:(ht + 1) * QW],
                                         start=(ht == 0), stop=(ht == NHT - 1))
                    m56c = C_["m56"].inc(mm_i)
                    w13_free[rr2] = m56c
                    if hhb + 2 < NHHT:
                        load_w13(hhb + 2)
                    sc.wait_ge(C_["m56"].sem, m56c)
                    if h1_free[par] > 0:
                        sc.wait_ge(C_["h1"].sem, h1_free[par])
                    sl = s_silu[:, par * 512:(par + 1) * 512]
                    svc = C_["sil"].inc(sc.activation(sl, pb[bk][:, :], ACT.Silu))
                    ve.wait_ge(C_["sil"].sem, svc)
                    ve.tensor_tensor(s_h1[:, hhb * QW:(hhb + 1) * QW],
                                     sl, pb[bk + 1][:, :], A.mult)
                    ve.drain()
                    h1_free[par] = C_["h1"].inc(ve.nop())
                te.wait_ge(w2ch.sem, w2_target)
                te.wait_ge(C_["w2s"].sem, w2s_target)
                te.wait_ge(C_["h1"].sem, C_["h1"].n)
                for kk in range(4):
                    tile = qq * 4 + kk
                    rr2 = kk % 2
                    if evac_hbm:
                        if oe_ring_free[rr2] > 0:
                            sc.wait_ge(C_["oewA" if rr2 == 0 else "oewB"].sem,
                                       oe_ring_free[rr2])
                    else:
                        if tile < 2:
                            sc.wait_ge(C_["oewA" if rr2 == 0 else "oewB"].sem,
                                       oew_final[rr2])
                        else:
                            sc.wait_ge(C_["yv"].sem, tile - 1)
                        if debug and tile >= 2:
                            sc.wait_ge(C_["dbgg"].sem, C_["dbgg"].n)
                    for hc in range(4):
                        par = hc % 2
                        if oe_bank_free[par] > 0:
                            te.wait_ge(C_["oev"].sem, oe_bank_free[par])
                        for hhb in range(NHHT):
                            mm_i = te.matmul(pb[4 + par][:, :],
                                             s_h1[:, hhb * QW + kk * 128: hhb * QW + (kk + 1) * 128],
                                             s_wbig[:, hhb * H + hc * 512: hhb * H + (hc + 1) * 512],
                                             start=(hhb == 0), stop=(hhb == NHHT - 1))
                        m7c = C_["m7"].inc(mm_i)
                        sc.wait_ge(C_["m7"].sem, m7c)
                        dst = (s_oe[rr2] if evac_hbm else s_oesr[rr2])
                        oe_bank_free[par] = C_["oev"].inc(sc.copy(
                            dst[:, hc * 512:(hc + 1) * 512], pb[4 + par][:, :]))
                    if evac_hbm:
                        sy.wait_ge(C_["oev"].sem, C_["oev"].n)
                        oe_ring_free[rr2] = C_["oewA" if rr2 == 0 else "oewB"].inc(
                            sy.dma_start(
                                out=out_stage[tile * 128:(tile + 1) * 128, :],
                                in_=s_oe[rr2][:, :]), 16)
                    else:
                        oes_ready[tile] = C_["oev"].n
                    if post_q is not None and tile >= 1:
                        post_q(tile - 1)
                    if build_next is not None and qq + 1 < NQ:
                        r = build_next(qq + 1, kk)
                        if kk == 3:
                            ready[qq + 1] = r
                    elif tail_build is not None and qq + 1 == NQ:
                        r = tail_build(0, kk)
                        if kk == 3:
                            tail_ready[0] = r
            if post_q is not None:
                post_q(4 * NQ - 1)
            return tail_ready[0]

        for _k in range(4):
            _r = gather_q(0, _k)
        ready5 = {0: _r}
        shared_r0 = ffn_phase(w1b, w3b, w2b, oe_stage, ready5, gather_q,
                              evac_hbm=True, tail_build=build_x1q, pre=pre_exp)
        if debug:
            gp.wait_ge(C_["oewA"].sem, C_["oewA"].n)
            gp.wait_ge(C_["oewB"].sem, C_["oewB"].n)
            C_["dbgg"].inc(gp.dma_start(out=dbg["oe"][:, :], in_=oe_stage[:, :]), 16)
            C_["dbgg"].inc(gp.dma_start(out=dbg["xd"][:, :], in_=x1_all[0:128, :]), 16)
        if stop_after == "P5":
            nc.all_engine_barrier()
            p4ctx.close()
            _finish(nc, sy, gp, C_)
            return nc, dbg

        gp.wait_ge(C_["oewA"].sem, C_["oewA"].n)
        gp.wait_ge(C_["oewB"].sem, C_["oewB"].n)
        C_["agoe"].inc(gp.collective_compute(
            "AllGather", A.bypass, replica_groups=[list(range(8))],
            ins=[oe_stage.ap().opt()], outs=[oe_all.ap().opt()]))

        ready4 = {0: shared_r0}
        ffn_phase(sw1b, sw3b, sw2b, None, ready4, build_x1q,
                  evac_hbm=False, post_q=post_tile, w2_wait_m7=True,
                  last_cast=True)

        sy.wait_ge(C_["ywA"].sem, C_["ywA"].n)
        sy.wait_ge(C_["ywB"].sem, C_["ywB"].n)
        if debug:
            sy.wait_ge(C_["dbgw"].sem, C_["dbgw"].n)
            gp.wait_ge(C_["dbgg"].sem, C_["dbgg"].n)
        p4ctx.close()
    return nc, dbg


def _finish(nc, sy, gp, C_):
    sy.wait_ge(C_["dbgw"].sem, C_["dbgw"].n)
    gp.wait_ge(C_["dbgg"].sem, C_["dbgg"].n)
    sy.wait_ge(C_["ywA"].sem, C_["ywA"].n)
    sy.wait_ge(C_["ywB"].sem, C_["ywB"].n)


_CACHE = {}


def _prep(inputs):
    x = np.ascontiguousarray(np.asarray(inputs["L0_attn_query"], dtype=np.float32))
    Wqkv = np.asarray(inputs["L0_attn_Wqkv"], dtype=np.float32)
    Wproj = np.asarray(inputs["L0_attn_Wproj"], dtype=np.float32)
    bproj = np.asarray(inputs["L0_attn_bproj"], dtype=np.float32)
    Wg = np.asarray(inputs["L0_ffn_Wg"], dtype=np.float32)
    W1 = np.asarray(inputs["L0_ffn_W1"], dtype=np.float32)
    W3 = np.asarray(inputs["L0_ffn_W3"], dtype=np.float32)
    W2 = np.asarray(inputs["L0_ffn_W2"], dtype=np.float32)
    sw1 = np.asarray(inputs["L0_ffn_sw1"], dtype=np.float32)
    sw3 = np.asarray(inputs["L0_ffn_sw3"], dtype=np.float32)
    sw2 = np.asarray(inputs["L0_ffn_sw2"], dtype=np.float32)

    xT = np.ascontiguousarray(x.transpose(0, 2, 1))
    wprojT_t = np.ascontiguousarray(Wproj.T.reshape(NHT, 128, H))
    wproj_nat_t = np.ascontiguousarray(Wproj.reshape(NHT, 128, H))
    bprojr = np.ascontiguousarray(bproj.reshape(1, H))
    bprojc = np.ascontiguousarray(bproj.reshape(NHT, 128).T)
    wg_t = np.ascontiguousarray(
        Wg.reshape(NHT, 128, E).transpose(1, 0, 2).reshape(128, NHT * E))
    sw1_t = np.ascontiguousarray(
        sw1.reshape(NHT, 128, NHHT, 128).transpose(2, 1, 0, 3).reshape(NHHT, 128, NHT * 128))
    sw3_t = np.ascontiguousarray(
        sw3.reshape(NHT, 128, NHHT, 128).transpose(2, 1, 0, 3).reshape(NHHT, 128, NHT * 128))
    sw2_t = np.ascontiguousarray(sw2.reshape(NHHT, 128, H))

    per_core = []
    for c in range(8):
        blocks = []
        for hd in (c, c + 8):
            base = hd * 3 * HD
            blocks.append(Wqkv[base:base + HD])
            blocks.append(Wqkv[base + HD:base + 2 * HD])
        for hd in (c, c + 8):
            base = hd * 3 * HD
            blocks.append(Wqkv[base + 2 * HD:base + 3 * HD])
        sub = np.stack(blocks, 0)
        wqkv_c = np.ascontiguousarray(
            sub.reshape(6, 128, NHT, 128).transpose(3, 2, 0, 1).reshape(128, NHT * 6 * 128))
        xTc_c = np.ascontiguousarray(
            x[c].reshape(NST, 128, NHT, 128).transpose(0, 3, 2, 1).reshape(NST, 128, H))
        w1_c = np.ascontiguousarray(
            W1[c].reshape(NHT, 128, NHHT, 128).transpose(2, 1, 0, 3).reshape(NHHT, 128, NHT * 128))
        w3_c = np.ascontiguousarray(
            W3[c].reshape(NHT, 128, NHHT, 128).transpose(2, 1, 0, 3).reshape(NHHT, 128, NHT * 128))
        w2_c = np.ascontiguousarray(W2[c].reshape(NHHT, 128, H))
        sel = np.zeros((128, NST), dtype=np.float32)
        for i in range(NST):
            sel[16 * c + i, i] = 1.0
        per_core.append({
            "xT": xT, "x_nat": np.ascontiguousarray(x[c]), "xTc": xTc_c,
            "wqkv": wqkv_c, "wprojT": wprojT_t, "wproj_nat": wproj_nat_t,
            "bprojr": bprojr, "bprojc": bprojc, "wg": wg_t,
            "w1": w1_c, "w3": w3_c, "w2": w2_c,
            "sw1": sw1_t, "sw3": sw3_t, "sw2": sw2_t,
            "sel16": sel, "myexp": np.full((128, 1), float(c), dtype=np.float32),
        })
    return per_core


def kernel(**inputs):
    per_core = _prep(inputs)
    if "nc" not in _CACHE:
        _CACHE["nc"] = build(debug=False)[0]
    nc = _CACHE["nc"]
    res = run_bass_kernel_spmd(nc, per_core, core_ids=list(range(8)))
    return np.stack([res.results[c]["y"] for c in range(8)], axis=0)


# version 20
# speedup vs baseline: 1.0128x; 1.0128x over previous
import numpy as np
import ml_dtypes
import concourse.bass as bass
import concourse.mybir as mybir
import concourse.library_config as libcfg
from concourse.bass_utils import run_bass_kernel_spmd
from concourse.bass_core_rust import IndirectOffsetOnAxis
from contextlib import ExitStack

F32 = mybir.dt.float32
BF16 = mybir.dt.bfloat16
I16 = mybir.dt.int16
U32 = mybir.dt.uint32
A = mybir.AluOpType
ACT = mybir.ActivationFunctionType
AX = mybir.AxisListType.X

B, S, H, NH, HD = 8, 2048, 2048, 16, 128
E, CAP, HHD = 8, 2048, 2816
NHT, NST, NHHT = 16, 16, 22
INV_SQRT_HD = float(1.0 / np.sqrt(np.float32(128.0)))
EXP_SHIFT = -4.0
EPS = 1e-5
CVALS = [0.0, 1.0, 2.0, 3.0, 4.0, 5.0, 6.0, 7.0,
         -1.0, EXP_SHIFT, float(CAP), 16383.0, EPS, 1024.0]


class Ch:
    def __init__(self, nc, ctx, name):
        self.sem = ctx.enter_context(nc.semaphore(name))
        self.n = 0

    def inc(self, inst, k=1):
        self.n += k
        inst.then_inc(self.sem, k)
        return self.n



def _ring2(En, nc, name, width, dt):
    buf = En(nc.sbuf_tensor(name, [128, 2 * width], dt))
    return [buf[:, 0:width], buf[:, width:2 * width]]

def build(debug=False, stop_after=None, shared_ag=True):
    nc = bass.Bass("TRN2", target_bir_lowering=False)
    dram = nc.dram_tensor

    xT = dram("xT", [B, H, S], F32, kind="ExternalInput")
    x_nat = dram("x_nat", [S, H], F32, kind="ExternalInput")
    xTc = dram("xTc", [NST, 128, H], F32, kind="ExternalInput")
    wqkv_d = dram("wqkv", [128, NHT * 6 * 128], F32, kind="ExternalInput")
    wprojT_d = dram("wprojT", [NHT, 128, H], F32, kind="ExternalInput")
    wproj_nat = dram("wproj_nat", [NHT, 128, H], F32, kind="ExternalInput")
    bprojr_d = dram("bprojr", [1, H], F32, kind="ExternalInput")
    bprojc_d = dram("bprojc", [128, NHT], F32, kind="ExternalInput")
    wg_d = dram("wg", [128, NHT * E], F32, kind="ExternalInput")
    w1_d = dram("w1", [NHHT, 128, NHT * 128], F32, kind="ExternalInput")
    w3_d = dram("w3", [NHHT, 128, NHT * 128], F32, kind="ExternalInput")
    w2_d = dram("w2", [NHHT, 128, H], F32, kind="ExternalInput")
    sw1_d = dram("sw1", [NHHT, 128, NHT * 128], F32, kind="ExternalInput")
    sw3_d = dram("sw3", [NHHT, 128, NHT * 128], F32, kind="ExternalInput")
    sw2_d = dram("sw2", [NHHT, 128, H], F32, kind="ExternalInput")
    sel_d = dram("sel16", [128, NST], F32, kind="ExternalInput")
    myexp_d = dram("myexp", [128, 1], F32, kind="ExternalInput")
    y = dram("y", [S, H], F32, kind="ExternalOutput")

    x1_stage = dram("x1_stage", [S, H], BF16)
    x1_all = dram("x1_all", [B * S, H], BF16, addr_space="Shared" if shared_ag else "Local")
    eidx_stage = dram("eidx_stage", [16, 128], F32)
    eidx_all = dram("eidx_all", [128, 128], F32, addr_space="Shared" if shared_ag else "Local")
    oe_stage = dram("oe_stage", [CAP, H], BF16)
    oe_all = dram("oe_all", [E * CAP, H], BF16, addr_space="Shared" if shared_ag else "Local")
    oes_stage = dram("oes_stage", [S, H], BF16)
    floc_hbm = dram("floc_hbm", [128, 128], F32)
    w1b = dram("w1b", [NHHT, 128, NHT * 128], BF16)
    w3b = dram("w3b", [NHHT, 128, NHT * 128], BF16)
    w2b = dram("w2b", [NHHT, 128, H], BF16)
    sw1b = dram("sw1b", [NHHT, 128, NHT * 128], BF16)
    sw3b = dram("sw3b", [NHHT, 128, NHT * 128], BF16)
    sw2b = dram("sw2b", [NHHT, 128, H], BF16)

    dbg = {}
    if debug:
        for nm, shp in [("ctxT", [128, NHT * S]), ("x1", [S, H]),
                        ("eidx", [16, 128]), ("gate", [128, NST]),
                        ("pos", [128, 128]), ("keep", [128, 128]),
                        ("stok", [128, 16]), ("gk", [128, NST]),
                        ("reti", [128, NST]), ("xd", [128, H]),
                        ("oes", [S, H]), ("oe", [CAP, H]),
                        ("eg", [128, 128])]:
            dbg[nm] = dram("dbg_" + nm, shp, F32, kind="ExternalOutput")

    id_f = nc.inline_tensor(np.eye(128, dtype=np.float32), "id_f")
    id_b = nc.inline_tensor(np.eye(128, dtype=ml_dtypes.bfloat16), "id_b")
    onescol_f = nc.inline_tensor(np.ones((128, 1), dtype=np.float32), "onescol_f")
    onescol_b = nc.inline_tensor(np.ones((128, 1), dtype=ml_dtypes.bfloat16), "onescol_b")
    onesrow_f = nc.inline_tensor(np.ones((1, 128), dtype=np.float32), "onesrow_f")
    tri_d = nc.inline_tensor(np.triu(np.ones((128, 128), dtype=np.float32), 1), "tri")
    shift1_d = nc.inline_tensor(np.eye(128, k=1).astype(np.float32), "shift1")
    iota8_d = nc.inline_tensor(
        np.tile(np.arange(8, dtype=np.float32)[None, :], (128, 1)), "iota8")
    big8_d = nc.inline_tensor(np.full((128, 8), 64.0, dtype=np.float32), "big8")
    iota8w_d = nc.inline_tensor(
        np.tile(np.arange(8, dtype=np.float32)[None, :], (128, 16)), "iota8w")
    big8w_d = nc.inline_tensor(np.full((128, 128), 64.0, dtype=np.float32), "big8w")
    consts_d = nc.inline_tensor(
        np.tile(np.array(CVALS, dtype=np.float32)[None, :], (128, 1)), "cvals")

    ctx = ExitStack()
    with ctx:
        En = ctx.enter_context

        s_consts = En(nc.sbuf_tensor("s_consts", [128, len(CVALS)], F32))
        for j, v in enumerate(CVALS):
            nc.const_aps.register(F32, v, s_consts[:, j:j + 1])
        s_idf = En(nc.sbuf_tensor("s_idf", [128, 128], F32))
        s_idb = En(nc.sbuf_tensor("s_idb", [128, 128], BF16))
        s_1cf = En(nc.sbuf_tensor("s_1cf", [128, 1], F32))
        s_1cb = En(nc.sbuf_tensor("s_1cb", [128, 1], BF16))
        s_1rf = En(nc.sbuf_tensor("s_1rf", [1, 128], F32))
        s_tri = En(nc.sbuf_tensor("s_tri", [128, 128], F32))
        s_io8 = En(nc.sbuf_tensor("s_io8", [128, 8], F32))
        s_big8 = En(nc.sbuf_tensor("s_big8", [128, 8], F32))
        s_sel = En(nc.sbuf_tensor("s_sel", [128, NST], F32))
        s_mye = En(nc.sbuf_tensor("s_mye", [128, 1], F32))
        s_wg = En(nc.sbuf_tensor("s_wg", [128, NHT * E], F32))
        s_bprojc = En(nc.sbuf_tensor("s_bprojc", [128, NHT], F32))
        s_G = En(nc.sbuf_tensor("s_G", [128, 8], F32))
        s_bwg = En(nc.sbuf_tensor("s_bwg", [128, 8], F32))
        s_wpb = En(nc.sbuf_tensor("s_wpb", [128, NHT * 8], BF16))
        s_gate = En(nc.sbuf_tensor("s_gate", [128, NST], F32))
        s_eidxb = En(nc.sbuf_tensor("s_eidxb", [128, NST], F32))
        s_gk = En(nc.sbuf_tensor("s_gk", [128, NST], F32))
        s_retiU = En(nc.sbuf_tensor("s_retiU", [128, NST], U32))
        s_stokU = En(nc.sbuf_tensor("s_stokU", [128, 16], U32))
        s_bn = En(nc.sbuf_tensor("s_bn", [128, 24], F32))
        s_mv = En(nc.sbuf_tensor("s_mv", [128, 2], F32))
        s_sig = En(nc.sbuf_tensor("s_sig", [128, 1], F32))
        s_rsig = En(nc.sbuf_tensor("s_rsig", [128, 1], F32))
        s_nmr = En(nc.sbuf_tensor("s_nmr", [128, 1], F32))
        s_rsig16 = En(nc.sbuf_tensor("s_rsig16", [128, 16], F32))
        s_nmr16 = En(nc.sbuf_tensor("s_nmr16", [128, 16], F32))
        s_tmp8r = En(nc.sbuf_tensor("s_tmp8r", [1, 8], F32))

        pbA = En(nc.psum_tensor("pbA", [128, 2048], F32))
        pbB = En(nc.psum_tensor("pbB", [128, 2048], F32))
        pbAb = pbA.ap().bitcast(BF16)
        pbBb = pbB.ap().bitcast(BF16)

        class _PSl:
            def __init__(self, t, c0, w):
                self.t, self.c0, self.w = t, c0, w

            def __getitem__(self, idx):
                p, f = idx
                a = f.start if f.start is not None else 0
                b = f.stop if f.stop is not None else self.w
                return self.t[p, self.c0 + a:self.c0 + b]

        pb = [_PSl(pbA if k < 4 else pbB, (k % 4) * 512, 512) for k in range(8)]
        pbb = [_PSl(pbAb if k < 4 else pbBb, (k % 4) * 1024, 1024) for k in range(8)]

        chnames = ["ld0", "wq", "xt", "wp", "wpc", "m1", "m1e", "m2", "exp",
                   "m3", "dcp", "rcp", "bc", "bce", "ctx", "xtc", "xres",
                   "m4", "tt", "m4r", "rt8", "lnv", "sqc", "x1v", "x1w",
                   "eix", "agx", "age", "eir", "rtv", "rtt", "rte", "stok",
                   "gx", "gx2", "w2s", "x1r", "tpo", "tpe", "w13", "w2l", "m56", "h1",
                   "m7", "sil", "oev", "oew", "agoe", "ret", "x1r2", "oesr",
                   "yv", "yw", "dbgw", "dbgg", "gexp", "xrdy", "dsum", "dsug",
                   "xt0", "xt1", "wpn0", "wpn1", "xres0", "xres1",
                   "xtc0", "xtc1", "x1w0", "x1w1", "x1rA", "x1rB",
                   "w13a", "w13b", "oewA", "oewB", "retA", "retB", "wcast",
                   "x1fA", "x1fB", "oesA", "oesB", "ywA", "ywB"]
        C_ = {name: Ch(nc, ctx, "ch_" + name) for name in chnames}

        sy, gp, te, ve, sc = nc.sync, nc.gpsimd, nc.tensor, nc.vector, nc.scalar

        for buf, src in [(s_consts, consts_d), (s_idf, id_f), (s_idb, id_b),
                         (s_1cf, onescol_f), (s_1cb, onescol_b), (s_1rf, onesrow_f),
                         (s_tri, tri_d), (s_io8, iota8_d), (s_big8, big8_d)]:
            C_["ld0"].inc(sy.dma_start(out=buf[:], in_=src.ap()), 16)
        for buf, src in [(s_sel, sel_d), (s_mye, myexp_d), (s_wg, wg_d),
                         (s_bprojc, bprojc_d)]:
            C_["ld0"].inc(sy.dma_start(out=buf[:], in_=src[:, :]), 16)
        ld0_done = C_["ld0"].n

        p0ctx = ctx.enter_context(ExitStack())
        E0 = p0ctx.enter_context
        s_wpT = E0(nc.sbuf_tensor("s_wpT", [8, H], F32))
        s_wpn = _ring2(E0, nc, "s_wpn", H, F32)

        wpn_at = []
        for t in range(NHT):
            if t >= 2:
                sy.wait_ge(C_["wpc"].sem, t - 1)
            ch = C_["wpn%d" % (t % 2)]
            wpn_at.append((ch, ch.inc(sy.dma_start(
                out=s_wpn[t % 2][:, :], in_=wproj_nat[t]), 16)))

        te.wait_ge(C_["ld0"].sem, ld0_done)
        for t in range(NHT):
            te.wait_ge(wpn_at[t][0].sem, wpn_at[t][1])
            for cc in range(4):
                mm_i = te.matmul(pb[cc][0:8, :], s_wg[:, t * 8:(t + 1) * 8],
                                 s_wpn[t % 2][:, cc * 512:(cc + 1) * 512],
                                 start=(t == 0), stop=(t == NHT - 1))
            C_["wpc"].inc(mm_i)
        for t in range(NHT):
            te.matmul(pb[4][0:1, 0:8], s_1cf[:], s_wg[:, t * 8:(t + 1) * 8],
                      start=(t == 0), stop=(t == NHT - 1))
        for t in range(NHT):
            mm_i = te.matmul(pb[4][0:1, 8:16], s_bprojc[:, t:t + 1],
                             s_wg[:, t * 8:(t + 1) * 8],
                             start=(t == 0), stop=(t == NHT - 1))
        m4c = C_["m4"].inc(mm_i)

        sc.wait_ge(C_["m4"].sem, m4c)
        for cc in range(4):
            sc_i = sc.copy(s_wpT[:, cc * 512:(cc + 1) * 512], pb[cc][0:8, :])
        sq1 = C_["sqc"].inc(sc_i)
        te.wait_ge(C_["sqc"].sem, sq1)
        for t in range(NHT):
            mm_i = te.transpose(pb[5][:, t * 8:(t + 1) * 8],
                                s_wpT[0:8, t * 128:(t + 1) * 128], s_idf[0:8, 0:8])
        m4c = C_["m4"].inc(mm_i)
        sc.wait_ge(C_["m4"].sem, m4c)
        sc.copy(s_wpb[:], pb[5][:, 0:NHT * 8])
        sq2 = C_["sqc"].inc(sc.copy(s_tmp8r[:], pb[4][0:1, 0:8]))
        te.wait_ge(C_["sqc"].sem, sq2)
        m4c = C_["m4"].inc(te.matmul(pb[6][:, 0:8], s_1rf[0:1, :],
                                      s_tmp8r[0:1, :], start=True, stop=True))
        sc.wait_ge(C_["m4"].sem, m4c)
        sc.copy(s_G[:], pb[6][:, 0:8])
        sq3 = C_["sqc"].inc(sc.copy(s_tmp8r[:], pb[4][0:1, 8:16]))
        te.wait_ge(C_["sqc"].sem, sq3)
        m4c = C_["m4"].inc(te.matmul(pb[6][:, 8:16], s_1rf[0:1, :],
                                      s_tmp8r[0:1, :], start=True, stop=True))
        sc.wait_ge(C_["m4"].sem, m4c)
        C_["sqc"].inc(sc.copy(s_bwg[:], pb[6][:, 8:16]))

        nc.all_engine_barrier()
        p0ctx.close()

        p12ctx = ctx.enter_context(ExitStack())
        s_ctxT = p12ctx.enter_context(nc.sbuf_tensor("s_ctxT", [128, NHT * S], BF16))
        p1ctx = ctx.enter_context(ExitStack())
        E1 = p1ctx.enter_context
        s_wqkv = E1(nc.sbuf_tensor("s_wqkv", [128, NHT * 6 * 128], BF16))
        s_xth = _ring2(E1, nc, "s_xth", NHT * 512, BF16)
        s_qkT = E1(nc.sbuf_tensor("s_qkT", [128, 4 * S], BF16))
        s_vb = E1(nc.sbuf_tensor("s_vb", [128, NHT * 256], BF16))
        s_expT = _ring2(E1, nc, "s_expT", NHT * 512, BF16)
        s_den = [E1(nc.sbuf_tensor("s_denA", [1, 512], F32)),
                 E1(nc.sbuf_tensor("s_denB", [1, 512], F32))]
        s_recip = [E1(nc.sbuf_tensor("s_recipA", [1, 512], F32)),
                   E1(nc.sbuf_tensor("s_recipB", [1, 512], F32))]
        s_rb = E1(nc.sbuf_tensor("s_rb", [128, 1024], F32))

        C_["wq"].inc(gp.dma_start(out=s_wqkv[:], in_=wqkv_d[:, :]), 16)

        xt_at = {}
        m1_free = {}

        cast_map = {"w1": (w1_d, w1b), "w3": (w3_d, w3b), "w2": (w2_d, w2b),
                    "sw1": (sw1_d, sw1b), "sw3": (sw3_d, sw3b),
                    "sw2": (sw2_d, sw2b)}
        cast_jobs = []
        for hhb in (0, 1):
            cast_jobs += [("w1", hhb), ("w3", hhb)]
        cast_jobs += [("w2", hhb) for hhb in range(NHHT)]
        for hhb in range(2, NHHT):
            cast_jobs += [("w1", hhb), ("w3", hhb)]
        for hhb in (0, 1):
            cast_jobs += [("sw1", hhb), ("sw3", hhb)]
        cast_jobs += [("sw2", hhb) for hhb in range(NHHT)]
        for hhb in range(2, NHHT):
            cast_jobs += [("sw1", hhb), ("sw3", hhb)]
        cast_pos = [0]
        wcast_final = 16 * len(cast_jobs)

        def drip_casts(k):
            for _ in range(k):
                if cast_pos[0] >= len(cast_jobs):
                    return
                nm, hhb = cast_jobs[cast_pos[0]]
                cast_pos[0] += 1
                fsrc, bdst = cast_map[nm]
                C_["wcast"].inc(gp.dma_start(out=bdst[hhb], in_=fsrc[hhb]), 16)

        def issue_xt(gq):
            par = gq % 2
            if gq - 2 >= 0 and (gq - 2) in m1_free:
                gp.wait_ge(C_["m1e"].sem, m1_free[gq - 2])
            j, qf = gq // 4, gq % 4
            ch = C_["xt%d" % par]
            for ht in range(NHT):
                ch.inc(gp.dma_start(
                    out=s_xth[par][:, ht * 512:(ht + 1) * 512],
                    in_=xT[j, ht * 128:(ht + 1) * 128, qf * 512:(qf + 1) * 512]), 16)
            xt_at[gq] = (ch, ch.n)

        issue_xt(0)
        issue_xt(1)
        drip_casts(8)
        te.wait_ge(C_["wq"].sem, 16)

        expT_reader = [0, 0]
        dacc_mm = [0, 0]
        ctx_bank_user = [0, 0]
        bce_user = [0, 0]

        pend = None

        def emit_bcast_ctx(p):
            eb_, ip_, qc_, dcp_n_ = p
            db_ = eb_ * 2 + 1
            te.wait_ge(C_["dcp"].sem, dcp_n_)
            bcc = C_["bc"].inc(te.matmul(
                pb[db_][:, :], s_1rf[0:1, :], s_recip[eb_][:, :],
                start=True, stop=True))
            sc.wait_ge(C_["bc"].sem, bcc)
            bce_user[eb_] = C_["bce"].inc(sc.copy(
                s_rb[:, eb_ * 512:eb_ * 512 + 512], pb[db_][:, :]))
            ve.wait_ge(C_["bce"].sem, bce_user[eb_])
            ctx_bank_user[eb_] = C_["ctx"].inc(ve.tensor_tensor(
                s_ctxT[:, ip_ * S + qc_ * 512: ip_ * S + (qc_ + 1) * 512],
                pb[eb_ * 2][:, :], s_rb[:, eb_ * 512:eb_ * 512 + 512], A.mult))

        pend_m3 = None

        def emit_prev_chunk(p, kg):
            eb_p, hh_p, ip_p, qc_p, expd = p
            cb_p, db_p = eb_p * 2, eb_p * 2 + 1
            if kg == 0:
                if ctx_bank_user[eb_p] > 0:
                    te.wait_ge(C_["ctx"].sem, ctx_bank_user[eb_p])
                te.wait_ge(C_["exp"].sem, expd)
            if kg < 4:
                for kt in range(kg * 4, kg * 4 + 4):
                    mm_i = te.matmul(
                        pb[cb_p][:, :],
                        s_vb[:, kt * 256 + hh_p * 128: kt * 256 + hh_p * 128 + 128],
                        s_expT[eb_p][:, kt * 512:(kt + 1) * 512],
                        start=(kt == 0), stop=(kt == NHT - 1))
                if kg == 3:
                    expT_reader[eb_p] = C_["m3"].inc(mm_i)
            else:
                if kg == 4 and bce_user[eb_p] > 0:
                    te.wait_ge(C_["bce"].sem, bce_user[eb_p])
                for kt in range((kg - 4) * 4, (kg - 4) * 4 + 4):
                    mm_i = te.matmul(pb[db_p][0:1, :], s_1cb[:],
                                     s_expT[eb_p][:, kt * 512:(kt + 1) * 512],
                                     start=(kt == 0), stop=(kt == NHT - 1))
                if kg == 7:
                    dacc_mm[eb_p] = C_["rtt"].inc(mm_i)

        def finish_prev(p):
            nonlocal pend
            eb_p, hh_p, ip_p, qc_p, expd = p
            db_p = eb_p * 2 + 1
            sc.wait_ge(C_["rtt"].sem, dacc_mm[eb_p])
            sc.activation(s_den[eb_p][:, :], pb[db_p][0:1, :], ACT.Ln)
            sc.drain()
            dcp_n = C_["dcp"].inc(sc.activation(
                s_recip[eb_p][:, :], s_den[eb_p][:, :], ACT.Exp, scale=-1.0))
            if pend is not None:
                emit_bcast_ctx(pend)
            pend = (eb_p, ip_p, qc_p, dcp_n)

        m1_groups = 0
        m1e_done = 0
        for j in range(B):
            for qf in range(4):
                gq = j * 4 + qf
                te.wait_ge(xt_at[gq][0].sem, xt_at[gq][1])
                xb = s_xth[gq % 2]
                te.wait_ge(C_["m1e"].sem, m1e_done)
                if qf == 0:
                    te.wait_ge(C_["ctx"].sem, C_["ctx"].n)
                    te.wait_ge(C_["bce"].sem, C_["bce"].n)
                for fi in range(4):
                    for ht in range(NHT):
                        w_ap = s_wqkv[:, (ht * 6 + fi) * 128:(ht * 6 + fi + 1) * 128]
                        mm_i = te.matmul(pb[fi][:, :], w_ap,
                                         xb[:, ht * 512:(ht + 1) * 512],
                                         start=(ht == 0), stop=(ht == NHT - 1))
                m1c = C_["m1"].inc(mm_i)
                ve.wait_ge(C_["m1"].sem, m1c)
                for fi in range(4):
                    ve.tensor_copy(s_qkT[:, fi * S + qf * 512:(fi * S + (qf + 1) * 512)],
                                   pb[fi][:, :])
                ve.drain()
                m1e_done = C_["m1e"].inc(ve.nop())
                if qf == 0:
                    te.wait_ge(C_["exp"].sem, C_["exp"].n)
                for tt in range(4):
                    for ht in range(NHT):
                        w_ap = s_wqkv[:, (ht * 6 + 4) * 128:(ht * 6 + 6) * 128]
                        mm_i = te.matmul(pb[4 + tt // 2][:, (tt % 2) * 256:(tt % 2) * 256 + 256],
                                         xb[:, ht * 512 + tt * 128: ht * 512 + (tt + 1) * 128],
                                         w_ap, start=(ht == 0), stop=(ht == NHT - 1))
                m1c = C_["m1"].inc(mm_i)
                ve.wait_ge(C_["m1"].sem, m1c)
                for tt in range(4):
                    gt = qf * 4 + tt
                    ve.tensor_copy(s_vb[:, gt * 256:(gt + 1) * 256],
                                   pb[4 + tt // 2][:, (tt % 2) * 256:(tt % 2) * 256 + 256])
                ve.drain()
                m1_free[gq] = C_["m1e"].inc(ve.nop())
                m1e_done = m1_free[gq]
                if gq + 2 < B * 4:
                    issue_xt(gq + 2)
                drip_casts(4)

            te.wait_ge(C_["m1e"].sem, C_["m1e"].n)

            for hh in range(2):
                iprime = 2 * j + hh
                qoff = (hh * 2) * S
                koff = (hh * 2 + 1) * S
                for qc in range(4):
                    eb = qc % 2
                    exp_base = C_["exp"].n
                    for kg in range(8):
                        for k2 in range(2):
                            kt = kg * 2 + k2
                            bank = 4 + (kg % 2) * 2 + k2
                            if k2 == 0 and exp_base + kg - 1 > 0:
                                te.wait_ge(C_["exp"].sem, exp_base + kg - 1)
                            m2c = C_["m2"].inc(te.matmul(
                                pb[bank][:, :],
                                s_qkT[:, koff + kt * 128: koff + (kt + 1) * 128],
                                s_qkT[:, qoff + qc * 512: qoff + (qc + 1) * 512],
                                start=True, stop=True))
                        sc.wait_ge(C_["m2"].sem, m2c)
                        if kg == 0 and expT_reader[eb] > 0:
                            sc.wait_ge(C_["rtt"].sem, dacc_mm[eb])
                        C_["exp"].inc(sc.activation(
                            s_expT[eb][:, kg * 1024:(kg + 1) * 1024],
                            pbB[:, (kg % 2) * 1024:(kg % 2) * 1024 + 1024],
                            ACT.Exp, bias=EXP_SHIFT, scale=INV_SQRT_HD))
                        if pend_m3 is not None:
                            emit_prev_chunk(pend_m3, kg)
                    if pend_m3 is not None:
                        finish_prev(pend_m3)
                    pend_m3 = (eb, hh, iprime, qc, C_["exp"].n)

            if pend_m3 is not None:
                for kg in range(8):
                    emit_prev_chunk(pend_m3, kg)
                finish_prev(pend_m3)
                pend_m3 = None
            if pend is not None:
                emit_bcast_ctx(pend)
                pend = None

        if debug:
            gp.wait_ge(C_["ctx"].sem, C_["ctx"].n)
            C_["dbgg"].inc(gp.dma_start(out=dbg["ctxT"][:, :], in_=s_ctxT[:]), 16)
            gp.wait_ge(C_["dbgg"].sem, C_["dbgg"].n)

        drip_casts(len(cast_jobs))
        nc.all_engine_barrier()
        p1ctx.close()
        if stop_after == "P1":
            _finish(nc, sy, gp, C_)
            return nc, dbg

        p2ctx = ctx.enter_context(ExitStack())
        E2 = p2ctx.enter_context
        s_wprojT = E2(nc.sbuf_tensor("s_wprojT", [128, NHT * H], BF16))
        s_bprojr = E2(nc.sbuf_tensor("s_bprojr", [1, H], F32))
        C_["ld0"].inc(sy.dma_start(out=s_bprojr[:], in_=bprojr_d[:, :]), 16)
        bprojr_ld = C_["ld0"].n
        s_xres = _ring2(E2, nc, "s_xres", H, F32)
        s_xtc = _ring2(E2, nc, "s_xtc", H, F32)
        s_t = _ring2(E2, nc, "s_t", H, F32)
        s_x1t = _ring2(E2, nc, "s_x1t", H, BF16)
        s_rt = E2(nc.sbuf_tensor("s_rt", [128, 128], F32))
        s_gn = E2(nc.sbuf_tensor("s_gn", [128, 128], F32))
        s_p8w = E2(nc.sbuf_tensor("s_p8w", [128, 128], F32))
        s_selw = E2(nc.sbuf_tensor("s_selw", [128, 128], F32))
        s_eqw = E2(nc.sbuf_tensor("s_eqw", [128, 128], mybir.dt.uint8))
        s_mx16 = E2(nc.sbuf_tensor("s_mx16", [128, 16], F32))
        s_Z16 = E2(nc.sbuf_tensor("s_Z16", [128, 16], F32))
        s_lnZ = E2(nc.sbuf_tensor("s_lnZ", [128, 16], F32))
        s_gsub = E2(nc.sbuf_tensor("s_gsub", [128, 16], F32))
        s_io8w = E2(nc.sbuf_tensor("s_io8w", [128, 128], F32))
        s_big8w = E2(nc.sbuf_tensor("s_big8w", [128, 128], F32))
        C_["rcp"].inc(sy.dma_start(out=s_io8w[:], in_=iota8w_d.ap()), 16)
        C_["rcp"].inc(sy.dma_start(out=s_big8w[:], in_=big8w_d.ap()), 16)
        io8w_ld = C_["rcp"].n

        for ht in range(NHT):
            C_["wp"].inc(gp.dma_start(
                out=s_wprojT[:, ht * H:(ht + 1) * H], in_=wprojT_d[ht]), 16)

        xres_at, xtc_at = {}, {}

        def p2_loads(st):
            r = st % 2
            if st >= 2:
                sy.wait_ge(C_["tt"].sem, st - 1)
            ch = C_["xres%d" % r]
            xres_at[st] = (ch, ch.inc(sy.dma_start(
                out=s_xres[r][:, :], in_=x_nat[st * 128:(st + 1) * 128, :]), 16))
            if st >= 2:
                sy.wait_ge(C_["m4r"].sem, st - 1)
            ch2 = C_["xtc%d" % r]
            xtc_at[st] = (ch2, ch2.inc(sy.dma_start(
                out=s_xtc[r][:, :], in_=xTc[st]), 16))

        p2_loads(0)
        p2_loads(1)
        for st in range(NST):
            r = st % 2
            if st + 2 < NST:
                p2_loads(st + 2)
            if st >= 1:
                te.wait_ge(C_["tt"].sem, st)
            if st == 0:
                te.wait_ge(C_["wp"].sem, 16 * NHT)
            for ht in range(NHT):
                l_ap = s_ctxT[:, ht * S + st * 128: ht * S + st * 128 + 128]
                for hc in range(4):
                    te.matmul(pb[hc][:, :], l_ap,
                              s_wprojT[:, ht * H + hc * 512: ht * H + (hc + 1) * 512],
                              start=(ht == 0), stop=False)
            if st == 0:
                te.wait_ge(C_["ld0"].sem, bprojr_ld)
            for hc in range(4):
                mm_i = te.matmul(pb[hc][:, :], s_1rf[0:1, :],
                                 s_bprojr[0:1, hc * 512:(hc + 1) * 512],
                                 start=False, stop=True)
            m4c = C_["m4"].inc(mm_i)
            if st >= 1:
                te.wait_ge(C_["rt8"].sem, st)
            te.wait_ge(xtc_at[st][0].sem, xtc_at[st][1])
            for ht in range(NHT):
                te.matmul(pb[6][:, 0:8],
                          s_ctxT[:, ht * S + st * 128: ht * S + st * 128 + 128],
                          s_wpb[:, ht * 8:(ht + 1) * 8],
                          start=(ht == 0), stop=False)
            for ht in range(NHT):
                mm_i = te.matmul(pb[6][:, 0:8],
                                 s_xtc[r][:, ht * 128:(ht + 1) * 128],
                                 s_wg[:, ht * 8:(ht + 1) * 8],
                                 start=False, stop=(ht == NHT - 1))
            m4rc = C_["m4r"].inc(mm_i)

            ve.wait_ge(C_["m4"].sem, m4c)
            ve.wait_ge(xres_at[st][0].sem, xres_at[st][1])
            ve.tensor_tensor(s_t[r][:, :], pbA[:, :], s_xres[r][:, :], A.add)
            ve.drain()
            C_["tt"].inc(ve.nop())
            for hc in range(4):
                ve.bn_stats(s_bn[:, hc * 6:(hc + 1) * 6],
                            s_t[r][:, hc * 512:(hc + 1) * 512])
            ve.drain()
            ve.bn_aggr(s_mv[:], s_bn[:])
            ve.drain()
            lnc = C_["lnv"].inc(ve.nop())
            sc.wait_ge(C_["lnv"].sem, lnc)
            sc.activation(s_sig[:], s_mv[:, 1:2], ACT.Ln, bias=EPS)
            sc.drain()
            sqc = C_["sqc"].inc(sc.activation(s_rsig16[:, st:st + 1], s_sig[:],
                                              ACT.Exp, scale=-0.5))
            ve.wait_ge(C_["sqc"].sem, sqc)
            ve.tensor_scalar(s_nmr16[:, st:st + 1], s_mv[:, 0:1],
                             s_rsig16[:, st:st + 1], -1.0,
                             op0=A.mult, op1=A.mult)
            ve.drain()
            nmc = C_["rtv"].inc(ve.nop())
            sc.wait_ge(C_["rtv"].sem, nmc)
            if st >= 2:
                sc.wait_ge(C_["x1w%d" % r].sem, 16 * (st // 2))
            sc.activation(s_x1t[r][:, :], s_t[r][:, :], ACT.Identity,
                          bias=s_nmr16[:, st:st + 1], scale=s_rsig16[:, st:st + 1])
            sc.drain()
            x1vc = C_["x1v"].inc(sc.nop())
            sy.wait_ge(C_["x1v"].sem, x1vc)
            C_["x1w%d" % r].inc(sy.dma_start(
                out=x1_stage[st * 128:(st + 1) * 128, :], in_=s_x1t[r][:, :]), 16)

            sc.wait_ge(C_["m4r"].sem, m4rc)
            C_["rt8"].inc(sc.copy(s_rt[:, st * 8:(st + 1) * 8], pb[6][:, 0:8]))

        rt3 = s_rt.ap().rearrange("p (s e) -> p s e", s=16)
        p83 = s_p8w.ap().rearrange("p (s e) -> p s e", s=16)
        sel3 = s_selw.ap().rearrange("p (s e) -> p s e", s=16)
        bwgB = s_bwg.ap().unsqueeze(1).broadcast_to([128, 16, 8])
        GB = s_G.ap().unsqueeze(1).broadcast_to([128, 16, 8])
        rsB = s_rsig16.ap().unsqueeze(2).broadcast_to([128, 16, 8])
        nmB = s_nmr16.ap().unsqueeze(2).broadcast_to([128, 16, 8])
        mxB = s_mx16.ap().unsqueeze(2).broadcast_to([128, 16, 8])
        ve.wait_ge(C_["rt8"].sem, C_["rt8"].n)
        ve.tensor_tensor(rt3, rt3, bwgB, A.add)
        ve.drain()
        ve.tensor_tensor(rt3, rt3, rsB, A.mult)
        ve.tensor_tensor(s_gn.ap().rearrange("p (s e) -> p s e", s=16),
                         GB, nmB, A.mult)
        ve.drain()
        ve.tensor_tensor(s_rt[:], s_rt[:], s_gn[:], A.add)
        ve.drain()
        ve.tensor_reduce(s_mx16.ap().unsqueeze(2), rt3, axis=AX, op=A.max)
        ve.drain()
        ve.tensor_tensor(s_eqw.ap().rearrange("p (s e) -> p s e", s=16),
                         rt3, mxB, A.is_ge)
        ve.drain()
        ve.wait_ge(C_["rcp"].sem, io8w_ld)
        ve.select(s_selw[:], s_eqw[:], s_io8w[:], s_big8w[:], add_drain=True)
        ve.drain()
        ve.tensor_reduce(s_eidxb.ap().unsqueeze(2), sel3, axis=AX, op=A.min)
        ve.drain()
        bvc = C_["rtv"].inc(ve.nop())
        sc.wait_ge(C_["rtv"].sem, bvc)
        sc.activation(s_p8w[:], s_rt[:], ACT.Exp)
        sc.drain()
        pvc = C_["sqc"].inc(sc.nop())
        ve.wait_ge(C_["sqc"].sem, pvc)
        ve.tensor_reduce(s_Z16.ap().unsqueeze(2), p83, axis=AX, op=A.add)
        ve.drain()
        zvc = C_["rtv"].inc(ve.nop())
        sc.wait_ge(C_["rtv"].sem, zvc)
        sc.activation(s_lnZ[:], s_Z16[:], ACT.Ln)
        sc.drain()
        lzc = C_["sqc"].inc(sc.nop())
        ve.wait_ge(C_["sqc"].sem, lzc)
        ve.tensor_tensor(s_gsub[:], s_mx16[:], s_lnZ[:], A.subtract)
        ve.drain()
        gvc = C_["rtv"].inc(ve.nop())
        sc.wait_ge(C_["rtv"].sem, gvc)
        C_["gexp"].inc(sc.activation(s_gate[:], s_gsub[:], ACT.Exp))

        if debug:
            gp.wait_ge(C_["x1w0"].sem, C_["x1w0"].n)
            gp.wait_ge(C_["x1w1"].sem, C_["x1w1"].n)
            C_["dbgg"].inc(gp.dma_start(out=dbg["x1"][:, :], in_=x1_stage[:, :]), 16)
            sy.wait_ge(C_["gexp"].sem, C_["gexp"].n)
            C_["dbgw"].inc(sy.dma_start(out=dbg["gate"][:, :], in_=s_gate[:]), 16)
            gp.wait_ge(C_["dbgg"].sem, C_["dbgg"].n)
            sy.wait_ge(C_["dbgw"].sem, C_["dbgw"].n)

        nc.all_engine_barrier()
        p2ctx.close()
        p12ctx.close()
        if stop_after == "P2":
            _finish(nc, sy, gp, C_)
            return nc, dbg

        NQ, QW = 4, 512
        mwctx = ctx.enter_context(ExitStack())
        EW = mwctx.enter_context
        s_wbig = EW(nc.sbuf_tensor("s_wbig", [128, NHHT * H], BF16))
        s_w13 = _ring2(EW, nc, "s_w13", 2 * H, BF16)
        w13chs = [C_["w13a"], C_["w13b"]]
        w2ch = C_["w2l"]

        def preload_ffn(w1src, w3src, w2src):
            pre = {}
            sy.wait_ge(C_["wcast"].sem, wcast_final)
            sc.wait_ge(C_["wcast"].sem, wcast_final)
            for hhb in (0, 1):
                eng = sy if hhb % 2 == 0 else sc
                ch = w13chs[hhb % 2]
                ch.inc(eng.dma_start(out=s_w13[hhb % 2][:, 0:H], in_=w1src[hhb]), 16)
                ch.inc(eng.dma_start(out=s_w13[hhb % 2][:, H:2 * H], in_=w3src[hhb]), 16)
                pre[hhb] = (ch, ch.n)
            for hhb in range(NHHT):
                w2ch.inc(sy.dma_start(
                    out=s_wbig[:, hhb * H:(hhb + 1) * H], in_=w2src[hhb]), 16)
            pre["w2_final"] = w2ch.n
            return pre

        p3ctx = ctx.enter_context(ExitStack())
        E3 = p3ctx.enter_context
        s_eT = E3(nc.sbuf_tensor("s_eT", [16, 128], F32))
        s_eG = E3(nc.sbuf_tensor("s_eG", [128, 128], F32))
        s_cmp8 = E3(nc.sbuf_tensor("s_cmp8", [128, 8 * 128], F32))
        s_scan8 = E3(nc.sbuf_tensor("s_scan8", [128, 8 * 128], F32))
        s_tot8 = E3(nc.sbuf_tensor("s_tot8", [128, 8], F32))
        s_off8 = E3(nc.sbuf_tensor("s_off8", [128, 8], F32))
        s_zero = E3(nc.sbuf_tensor("s_zero", [128, 128], F32))
        s_neg1 = E3(nc.sbuf_tensor("s_neg1", [128, 128], F32))
        s_pos = E3(nc.sbuf_tensor("s_pos", [128, 128], F32))
        s_ptmp = E3(nc.sbuf_tensor("s_ptmp", [128, 128], F32))
        s_keep = E3(nc.sbuf_tensor("s_keep", [128, 128], F32))
        s_flat = E3(nc.sbuf_tensor("s_flat", [128, 128], F32))
        s_cmpmy = E3(nc.sbuf_tensor("s_cmpmy", [128, 128], F32))
        s_iotaB = E3(nc.sbuf_tensor("s_iotaB", [128, 2048], F32))
        s_iotaBi = E3(nc.sbuf_tensor("s_iotaBi", [128, 2048], mybir.dt.int32))
        s_iotaC = E3(nc.sbuf_tensor("s_iotaC", [128, 16], F32))
        s_iotaCi = E3(nc.sbuf_tensor("s_iotaCi", [128, 16], mybir.dt.int32))
        s_scanMy = E3(nc.sbuf_tensor("s_scanMy", [128, 128], F32))
        s_totMy = E3(nc.sbuf_tensor("s_totMy", [128, 1], F32))
        s_offMy = E3(nc.sbuf_tensor("s_offMy", [128, 1], F32))
        s_doff = E3(nc.sbuf_tensor("s_doff", [128, 1], F32))
        s_flocal = E3(nc.sbuf_tensor("s_flocal", [128, 128], F32))
        s_big = E3(nc.sbuf_tensor("s_big", [128, 128 * 128], BF16))
        s_cmpJ = E3(nc.sbuf_tensor("s_cmpJ", [128, 128], F32))
        s_pA = E3(nc.sbuf_tensor("s_pA", [128, 16], F32))
        s_pB = E3(nc.sbuf_tensor("s_pB", [128, 16], F32))
        s_kof = E3(nc.sbuf_tensor("s_kof", [128, 16], F32))
        s_pou = E3(nc.sbuf_tensor("s_pou", [128, 16], U32))
        s_fl = E3(nc.sbuf_tensor("s_fl", [128, 128], F32))
        s_oneh = E3(nc.sbuf_tensor("s_oneh", [128, 128], F32))
        s_fof = E3(nc.sbuf_tensor("s_fof", [128, 1], F32))
        s_shift1 = E3(nc.sbuf_tensor("s_shift1", [128, 128], F32))
        s_ko = E3(nc.sbuf_tensor("s_ko", [16, 128], F32))
        s_fo = E3(nc.sbuf_tensor("s_fo", [16, 128], F32))
        s_keepT = E3(nc.sbuf_tensor("s_keepT", [128, 16], F32))
        s_stokF = E3(nc.sbuf_tensor("s_stokF", [128, 16], F32))
        s_retF = E3(nc.sbuf_tensor("s_retF", [128, 16], F32))

        eitc = C_["rtt"].inc(te.transpose(pb[0][0:16, 0:128], s_eidxb[:], s_idf[:]))
        sc.wait_ge(C_["rtt"].sem, eitc)
        eisc = C_["rte"].inc(sc.copy(s_eT[:], pb[0][0:16, 0:128]))
        gp.wait_ge(C_["rte"].sem, eisc)
        C_["eix"].inc(gp.dma_start(out=eidx_stage[:, :], in_=s_eT[:]), 16)
        gp.wait_ge(C_["eix"].sem, 16)
        C_["age"].inc(gp.collective_compute(
            "AllGather", A.bypass, replica_groups=[list(range(8))],
            ins=[eidx_stage.ap().opt()], outs=[eidx_all.ap().opt()]))
        gp.wait_ge(C_["x1w0"].sem, C_["x1w0"].n)
        gp.wait_ge(C_["x1w1"].sem, C_["x1w1"].n)
        C_["agx"].inc(gp.collective_compute(
            "AllGather", A.bypass, replica_groups=[list(range(8))],
            ins=[x1_stage.ap().opt()], outs=[x1_all.ap().opt()]))
        pre_exp = preload_ffn(w1b, w3b, w2b)
        gp.wait_ge(C_["age"].sem, 1)
        eirc = C_["eir"].inc(gp.dma_start(out=s_eG[:], in_=eidx_all[:, :]), 16)
        C_["ld0"].inc(sy.dma_start(out=s_shift1[:], in_=shift1_d.ap()), 16)
        C_["xt"].inc(gp.iota(s_iotaBi[:], [[1, 2048]], base=0,
                             channel_multiplier=0,
                             allow_small_or_imprecise_dtypes=True))
        C_["xt"].inc(gp.iota(s_iotaCi[:], [[128, 16]], base=0,
                             channel_multiplier=1,
                             allow_small_or_imprecise_dtypes=True))
        shift1_ld = C_["ld0"].n

        ve.memset(s_zero[:], 0.0)
        ve.memset(s_neg1[:], -1.0)
        ve.memset(s_pos[:], 0.0)
        ve.drain()
        ve.wait_ge(C_["xt"].sem, C_["xt"].n)
        ve.tensor_copy(s_iotaB[:], s_iotaBi[:])
        ve.tensor_copy(s_iotaC[:], s_iotaCi[:])
        ve.drain()
        ve.wait_ge(C_["eir"].sem, eirc)
        for e in range(8):
            ve.tensor_scalar(s_cmp8[:, e * 128:(e + 1) * 128], s_eG[:], float(e),
                             None, op0=A.is_equal)
        ve.drain()
        for e in range(8):
            ve.tensor_tensor_scan(s_scan8[:, e * 128:(e + 1) * 128],
                                  s_cmp8[:, e * 128:(e + 1) * 128], s_zero[:],
                                  initial=0.0, op0=A.add, op1=A.add)
        ve.drain()
        for e in range(8):
            ve.tensor_copy(s_tot8[:, e:e + 1],
                           s_scan8[:, (e + 1) * 128 - 1:(e + 1) * 128])
        ve.drain()
        rv1 = C_["rtv"].inc(ve.nop())
        te.wait_ge(C_["rtv"].sem, rv1)
        rtt1 = C_["rtt"].inc(te.matmul(pb[1][:, 0:8], s_tri[:], s_tot8[:],
                                       start=True, stop=True))
        ve.wait_ge(C_["rtt"].sem, rtt1)
        ve.tensor_copy(s_off8[:], pb[1][:, 0:8])
        ve.drain()
        for e in range(8):
            ve.tensor_scalar(s_ptmp[:], s_scan8[:, e * 128:(e + 1) * 128],
                             s_off8[:, e:e + 1], -1.0, op0=A.add, op1=A.add)
            ve.drain()
            ve.tensor_tensor(s_ptmp[:], s_ptmp[:],
                             s_cmp8[:, e * 128:(e + 1) * 128], A.mult)
            ve.drain()
            ve.tensor_tensor(s_pos[:], s_pos[:], s_ptmp[:], A.add)
            ve.drain()
        ve.tensor_scalar(s_keep[:], s_pos[:], float(CAP), None, op0=A.is_lt)
        ve.tensor_scalar(s_flat[:], s_eG[:], float(CAP), None, op0=A.mult)
        ve.drain()
        ve.tensor_tensor(s_flat[:], s_flat[:], s_pos[:], A.add)
        ve.drain()
        ve.tensor_scalar(s_flat[:], s_flat[:], 16383.0, None, op0=A.min)
        ve.tensor_scalar(s_cmpmy[:], s_eG[:], s_mye[:], None, op0=A.is_equal)
        ve.drain()
        ve.tensor_tensor_scan(s_scanMy[:], s_cmpmy[:], s_zero[:],
                              initial=0.0, op0=A.add, op1=A.add)
        ve.drain()
        ve.tensor_copy(s_totMy[:], s_scanMy[:, 127:128])
        ve.drain()
        rv2 = C_["rtv"].inc(ve.nop())
        te.wait_ge(C_["rtv"].sem, rv2)
        te.wait_ge(C_["ld0"].sem, shift1_ld)
        te.matmul(pb[2][:, 0:1], s_tri[:], s_totMy[:], start=True, stop=True)
        rtt2a = C_["rtt"].inc(te.matmul(pb[2][:, 1:2], s_shift1[:], s_totMy[:],
                                        start=True, stop=True))
        ve.wait_ge(C_["rtt"].sem, rtt2a)
        ve.tensor_copy(s_offMy[:], pb[2][:, 0:1])
        ve.tensor_copy(s_doff[:], pb[2][:, 1:2])
        ve.drain()
        big3 = s_big.ap().rearrange("p (k f) -> p k f", k=128)
        iotaB3 = s_iotaB.ap()[:, 0:128].unsqueeze(2).broadcast_to([128, 128, 128])
        scanB3 = s_scanMy.ap().unsqueeze(1).broadcast_to([128, 128, 128])
        ve.tensor_tensor(big3, iotaB3, scanB3, A.is_ge)
        ve.drain()
        ve.tensor_reduce(s_flocal.ap().unsqueeze(2), big3, axis=AX, op=A.add)
        ve.drain()
        rv3 = C_["rtv"].inc(ve.nop())
        gp.wait_ge(C_["rtv"].sem, rv3)
        flw = C_["stok"].inc(gp.dma_start(out=floc_hbm[:, :], in_=s_flocal[:]), 16)
        te.wait_ge(C_["rtv"].sem, rv3)
        for J in range(16):
            ve.tensor_scalar(s_cmpJ[:], s_iotaB[:, J * 128:(J + 1) * 128],
                             s_offMy[:], None, op0=A.is_ge)
            ve.drain()
            rvJ = C_["rtv"].inc(ve.nop())
            te.wait_ge(C_["rtv"].sem, rvJ)
            C_["rtt"].inc(te.matmul(pb[3][:, J:J + 1], s_cmpJ[:], s_1cf[:],
                                    start=True, stop=True))
            mmJ = C_["rtt"].inc(te.matmul(pb[4][:, J:J + 1], s_cmpJ[:], s_doff[:],
                                          start=True, stop=True))
            ve.wait_ge(C_["rtt"].sem, mmJ)
        ve.tensor_copy(s_pA[:], pb[3][:, 0:16])
        ve.tensor_copy(s_pB[:], pb[4][:, 0:16])
        ve.drain()
        ve.tensor_scalar(s_pA[:], s_pA[:], -1.0, None, op0=A.add)
        ve.tensor_tensor(s_kof[:], s_iotaC[:], s_pB[:], A.subtract)
        ve.drain()
        ve.tensor_copy(s_pou[:], s_pA[:])
        ve.drain()
        rv3b = C_["rtv"].inc(ve.nop())
        gp.wait_ge(C_["stok"].sem, flw)
        gp.wait_ge(C_["rtv"].sem, rv3b)
        for J in range(16):
            gxc = C_["gx"].inc(gp.indirect_dma_start(
                out=s_fl[:], out_offset=None, in_=floc_hbm[:, :],
                in_offset=IndirectOffsetOnAxis(s_pou[:, J:J + 1], axis=0)), 16)
            ve.wait_ge(C_["gx"].sem, gxc)
            ve.tensor_scalar(s_oneh[:], s_iotaB[:, 0:128], s_kof[:, J:J + 1],
                             None, op0=A.is_equal)
            ve.drain()
            ve.tensor_tensor(s_oneh[:], s_oneh[:], s_fl[:], A.mult)
            ve.drain()
            ve.reduce_sum(s_fof[:], s_oneh[:], axis=AX)
            ve.drain()
            ve.tensor_scalar(s_stokF[:, J:J + 1], s_pA[:, J:J + 1], 128.0,
                             s_fof[:, 0:1], op0=A.mult, op1=A.add)
            ve.drain()
            ve.tensor_scalar(s_stokF[:, J:J + 1], s_stokF[:, J:J + 1], 16383.0,
                             None, op0=A.min)
            ve.drain()
            gvJ = C_["rtv"].inc(ve.nop())
            gp.wait_ge(C_["rtv"].sem, gvJ)
        ve.tensor_copy(s_stokU[:], s_stokF[:])
        ve.drain()
        rv3c = C_["rtv"].inc(ve.nop())
        te.wait_ge(C_["rtv"].sem, rv3c)
        te.matmul(pb[3][0:16, 0:128], s_sel[:], s_keep[:], start=True, stop=True)
        rtt2 = C_["rtt"].inc(te.matmul(pb[4][0:16, 0:128], s_sel[:], s_flat[:],
                                       start=True, stop=True))
        sc.wait_ge(C_["rtt"].sem, rtt2)
        sc.copy(s_ko[:], pb[3][0:16, 0:128])
        rte2 = C_["rte"].inc(sc.copy(s_fo[:], pb[4][0:16, 0:128]))
        te.wait_ge(C_["rte"].sem, rte2)
        te.transpose(pb[5][:, 0:16], s_ko[:], s_idf[0:16, 0:16])
        rtt3 = C_["rtt"].inc(te.transpose(pb[6][:, 0:16], s_fo[:],
                                          s_idf[0:16, 0:16]))
        ve.wait_ge(C_["rtt"].sem, rtt3)
        ve.tensor_copy(s_keepT[:], pb[5][:, 0:16])
        ve.tensor_copy(s_retF[:], pb[6][:, 0:16])
        ve.drain()
        ve.wait_ge(C_["gexp"].sem, C_["gexp"].n)
        ve.tensor_tensor(s_gk[:], s_gate[:], s_keepT[:], A.mult)
        ve.drain()
        ve.tensor_copy(s_retiU[:], s_retF[:])
        ve.drain()
        rv4 = C_["rtv"].inc(ve.nop())

        if debug:
            sy.wait_ge(C_["rtv"].sem, rv4)
            C_["dbgw"].inc(sy.dma_start(out=dbg["eidx"][:, :], in_=eidx_stage[:, :]), 16)
            C_["dbgw"].inc(sy.dma_start(out=dbg["pos"][:, :], in_=s_pos[:]), 16)
            C_["dbgw"].inc(sy.dma_start(out=dbg["keep"][:, :], in_=s_keep[:]), 16)
            C_["dbgw"].inc(sy.dma_start(out=dbg["stok"][:, :], in_=s_stokF[:]), 16)
            C_["dbgw"].inc(sy.dma_start(out=dbg["gk"][:, :], in_=s_gk[:]), 16)
            C_["dbgw"].inc(sy.dma_start(out=dbg["reti"][:, :], in_=s_retF[:]), 16)
            C_["dbgw"].inc(sy.dma_start(out=dbg["eg"][:, :], in_=s_eG[:]), 16)
            sy.wait_ge(C_["dbgw"].sem, C_["dbgw"].n)

        nc.all_engine_barrier()
        p3ctx.close()
        if stop_after == "P3":
            _finish(nc, sy, gp, C_)
            return nc, dbg

        p4ctx = ctx.enter_context(ExitStack())
        E4 = p4ctx.enter_context
        arena = E4(nc.sbuf_tensor("arena", [128, 2 * NHT * QW + 2 * 2048], BF16))
        s_h1 = E4(nc.sbuf_tensor("s_h1", [128, NHHT * QW], BF16))
        s_silu = E4(nc.sbuf_tensor("s_silu", [128, 1024], F32))
        s_oe = _ring2(E4, nc, "s_oe", H, BF16)
        s_x1rd = _ring2(E4, nc, "s_x1rd", H, BF16)
        s_oesr = s_oe
        s_moe = E4(nc.sbuf_tensor("s_moe", [128, H], BF16))
        s_x1f = E4(nc.sbuf_tensor("s_x1f", [128, H], BF16))
        s_t2 = E4(nc.sbuf_tensor("s_t2", [128, H], F32))
        Rh = [arena[:, 0:NHT * QW], arena[:, NHT * QW:2 * NHT * QW]]
        s_xdt = [arena[:, 2 * NHT * QW:2 * NHT * QW + 2048],
                 arena[:, 2 * NHT * QW + 2048:2 * NHT * QW + 2 * 2048]]

        last_tpe = [None]

        def transpose_tiles_to_R(qq, k, src):
            for grpi, bank in ((0, 6), (1, 7)):
                if C_["tpo"].n >= 2:
                    te.wait_ge(C_["tpe"].sem, C_["tpo"].n - 1)
                for h2 in range(8):
                    ht = grpi * 8 + h2
                    mm_i = te.transpose(pbb[bank][:, h2 * 128:(h2 + 1) * 128],
                                        src[:, ht * 128:(ht + 1) * 128], s_idb[:])
                tpc = C_["tpo"].inc(mm_i)
                sc.wait_ge(C_["tpo"].sem, tpc)
                for h2 in range(8):
                    ht = grpi * 8 + h2
                    sc_i = sc.copy(Rh[qq % 2][:, ht * QW + k * 128: ht * QW + (k + 1) * 128],
                                   pbb[bank][:, h2 * 128:(h2 + 1) * 128])
                last_tpe[0] = sc_i
                C_["tpe"].inc(sc_i)

        x1q_steps = [0]
        xdt_steps = [0]

        def build_x1q(qq, k):
            st = qq * 4 + k
            rr = k % 2
            if x1q_steps[0] >= 2:
                gp.wait_ge(C_["tpe"].sem, C_["tpe"].n - 2)
            x1q_steps[0] += 1
            ch = C_["x1rA" if rr == 0 else "x1rB"]
            x1rc = ch.inc(gp.dma_start(
                out=s_x1rd[rr][:, :],
                in_=x1_stage[st * 128:(st + 1) * 128, :]), 16)
            te.wait_ge(ch.sem, x1rc)
            transpose_tiles_to_R(qq, k, s_x1rd[rr])
            return C_["tpe"].n

        def gather_q(qq, k):
            jj = qq * 4 + k
            rr = k % 2
            gp.wait_ge(C_["agx"].sem, 1)
            if xdt_steps[0] >= 2:
                gp.wait_ge(C_["tpe"].sem, C_["tpe"].n - 2)
            xdt_steps[0] += 1
            ch = C_["gx" if rr == 0 else "gx2"]
            gxc = ch.inc(gp.indirect_dma_start(
                out=s_xdt[rr], out_offset=None, in_=x1_all[:, :],
                in_offset=IndirectOffsetOnAxis(s_stokU[:, jj:jj + 1], axis=0)), 16)
            te.wait_ge(ch.sem, gxc)
            transpose_tiles_to_R(qq, k, s_xdt[rr])
            return C_["tpe"].n

        ret_at, x1f_at = {}, {}
        oes_ready = {}
        dbg_base = [0]

        def p6_loads(st):
            if st == 0:
                gp.wait_ge(C_["agoe"].sem, 1)
            if st >= 1:
                gp.wait_ge(C_["yv"].sem, st)
            ch = C_["retA" if st % 2 == 0 else "retB"]
            ret_at[st] = (ch, ch.inc(gp.indirect_dma_start(
                out=s_moe[:, :], out_offset=None, in_=oe_all[:, :],
                in_offset=IndirectOffsetOnAxis(s_retiU[:, st:st + 1], axis=0)), 16))
            if st >= 1:
                gp.wait_ge(C_["yv"].sem, st)
            ch = C_["x1fA" if st % 2 == 0 else "x1fB"]
            x1f_at[st] = (ch, ch.inc(gp.dma_start(
                out=s_x1f[:, :], in_=x1_stage[st * 128:(st + 1) * 128, :]), 16))

        def p6_stile(st):
            rr = st % 2
            ve.wait_ge(ret_at[st][0].sem, ret_at[st][1])
            ve.wait_ge(x1f_at[st][0].sem, x1f_at[st][1])
            ve.wait_ge(C_["oev"].sem, oes_ready[st])
            if st >= 1:
                pch = "ywA" if (st - 1) % 2 == 0 else "ywB"
                ve.wait_ge(C_[pch].sem, 16 * ((st - 1) // 2 + 1))
            ve.scalar_tensor_tensor(s_t2[:, :], s_moe[:, :],
                                    s_gk[:, st:st + 1], s_x1f[:, :],
                                    op0=A.mult, op1=A.add)
            ve.drain()
            ve.tensor_tensor(s_t2[:, :], s_t2[:, :], s_oesr[rr][:, :], A.add)
            ve.drain()
            for hc in range(4):
                ve.bn_stats(s_bn[:, hc * 6:(hc + 1) * 6],
                            s_t2[:, hc * 512:(hc + 1) * 512])
            ve.drain()
            ve.bn_aggr(s_mv[:], s_bn[:])
            ve.drain()
            lnc = C_["lnv"].inc(ve.nop())
            sc.wait_ge(C_["lnv"].sem, lnc)
            sc.activation(s_sig[:], s_mv[:, 1:2], ACT.Ln, bias=EPS)
            sc.drain()
            sqc = C_["sqc"].inc(sc.activation(s_rsig[:], s_sig[:], ACT.Exp, scale=-0.5))
            ve.wait_ge(C_["sqc"].sem, sqc)
            ve.tensor_scalar(s_nmr[:], s_mv[:, 0:1], s_rsig[:], -1.0,
                             op0=A.mult, op1=A.mult)
            ve.drain()
            ve.tensor_scalar(s_t2[:, :], s_t2[:, :], s_rsig[:], s_nmr[:],
                             op0=A.mult, op1=A.add)
            ve.drain()
            yvc = C_["yv"].inc(ve.nop())
            sy.wait_ge(C_["yv"].sem, yvc)
            C_["ywA" if rr == 0 else "ywB"].inc(sy.dma_start(
                out=y[st * 128:(st + 1) * 128, :], in_=s_t2[:, :]), 16)

        def post_tile(st):
            p6_loads(st)
            p6_stile(st)
            if debug:
                gp.wait_ge(C_["oev"].sem, oes_ready[st])
                gp.wait_ge(C_["dbgg"].sem, C_["dbgg"].n)
                C_["dbgg"].inc(gp.dma_start(
                    out=dbg["oes"][st * 128:(st + 1) * 128, :],
                    in_=s_oesr[st % 2][:, :]), 16)

        h1_free = [0, 0]
        w13_free = [0, 0]
        oe_bank_free = [0, 0]
        oe_ring_free = [0, 0]

        def ffn_phase(w1src, w3src, w2src, out_stage, ready, build_next,
                      evac_hbm=True, post_q=None, tail_build=None,
                      w2_wait_m7=False, pre=None, last_cast=False):
            if pre:
                w2_target = pre["w2_final"]
                w2s_target = C_["w2s"].n
            else:
                w2_target = w2ch.n + 16 * ((NHHT + 1) // 2)
                w2s_target = C_["w2s"].n + 16 * (NHHT // 2)
            m7_entry = C_["m7"].n
            if last_cast:
                sy.wait_ge(C_["wcast"].sem, wcast_final)
                sc.wait_ge(C_["wcast"].sem, wcast_final)
            oew_final = [C_["oewA"].n, C_["oewB"].n]
            if not evac_hbm:
                dbg_base[0] = C_["dbgg"].n
            tail_ready = [None]
            for qq in range(NQ):
                w13_at = {}

                def load_w13(hhb, qq=qq):
                    rr2 = hhb % 2
                    eng = sy if rr2 == 0 else sc
                    if w13_free[rr2] > 0:
                        eng.wait_ge(C_["m56"].sem, w13_free[rr2])
                    w13ch = w13chs[rr2]
                    w13ch.inc(eng.dma_start(out=s_w13[rr2][:, 0:H], in_=w1src[hhb]), 16)
                    w13ch.inc(eng.dma_start(out=s_w13[rr2][:, H:2 * H], in_=w3src[hhb]), 16)
                    w13_at[hhb] = (w13ch, w13ch.n)
                    if qq == 0 and pre is None:
                        if hhb == 0 and w2_wait_m7:
                            sy.wait_ge(C_["m7"].sem, m7_entry)
                        if hhb % 2 == 0:
                            w2ch.inc(sy.dma_start(
                                out=s_wbig[:, hhb * H:(hhb + 1) * H],
                                in_=w2src[hhb]), 16)
                        else:
                            C_["w2s"].inc(sc.dma_start(
                                out=s_wbig[:, hhb * H:(hhb + 1) * H],
                                in_=w2src[hhb]), 16)

                if qq == 0 and pre is not None:
                    w13_at[0] = pre[0]
                    w13_at[1] = pre[1]
                else:
                    load_w13(0)
                    load_w13(1)
                te.wait_ge(C_["tpe"].sem, ready[qq])
                for hhb in range(NHHT):
                    rr2 = hhb % 2
                    par = hhb % 2
                    te.wait_ge(w13_at[hhb][0].sem, w13_at[hhb][1])
                    if h1_free[par] > 0:
                        te.wait_ge(C_["h1"].sem, h1_free[par])
                    bk = par * 2
                    for ht in range(NHT):
                        te.matmul(pb[bk][:, :],
                                  s_w13[rr2][:, ht * 128:(ht + 1) * 128],
                                  Rh[qq % 2][:, ht * QW:(ht + 1) * QW],
                                  start=(ht == 0), stop=(ht == NHT - 1))
                    for ht in range(NHT):
                        mm_i = te.matmul(pb[bk + 1][:, :],
                                         s_w13[rr2][:, H + ht * 128: H + (ht + 1) * 128],
                                         Rh[qq % 2][:, ht * QW:(ht + 1) * QW],
                                         start=(ht == 0), stop=(ht == NHT - 1))
                    m56c = C_["m56"].inc(mm_i)
                    w13_free[rr2] = m56c
                    if hhb + 2 < NHHT:
                        load_w13(hhb + 2)
                    sc.wait_ge(C_["m56"].sem, m56c)
                    if h1_free[par] > 0:
                        sc.wait_ge(C_["h1"].sem, h1_free[par])
                    sl = s_silu[:, par * 512:(par + 1) * 512]
                    svc = C_["sil"].inc(sc.activation(sl, pb[bk][:, :], ACT.Silu))
                    ve.wait_ge(C_["sil"].sem, svc)
                    ve.tensor_tensor(s_h1[:, hhb * QW:(hhb + 1) * QW],
                                     sl, pb[bk + 1][:, :], A.mult)
                    ve.drain()
                    h1_free[par] = C_["h1"].inc(ve.nop())
                te.wait_ge(w2ch.sem, w2_target)
                te.wait_ge(C_["w2s"].sem, w2s_target)
                te.wait_ge(C_["h1"].sem, C_["h1"].n)
                for kk in range(4):
                    tile = qq * 4 + kk
                    rr2 = kk % 2
                    if evac_hbm:
                        if oe_ring_free[rr2] > 0:
                            sc.wait_ge(C_["oewA" if rr2 == 0 else "oewB"].sem,
                                       oe_ring_free[rr2])
                    else:
                        if tile < 2:
                            sc.wait_ge(C_["oewA" if rr2 == 0 else "oewB"].sem,
                                       oew_final[rr2])
                        else:
                            sc.wait_ge(C_["yv"].sem, tile - 1)
                        if debug and tile >= 2:
                            sc.wait_ge(C_["dbgg"].sem, C_["dbgg"].n)
                    for hc in range(4):
                        par = hc % 2
                        if oe_bank_free[par] > 0:
                            te.wait_ge(C_["oev"].sem, oe_bank_free[par])
                        for hhb in range(NHHT):
                            mm_i = te.matmul(pb[4 + par][:, :],
                                             s_h1[:, hhb * QW + kk * 128: hhb * QW + (kk + 1) * 128],
                                             s_wbig[:, hhb * H + hc * 512: hhb * H + (hc + 1) * 512],
                                             start=(hhb == 0), stop=(hhb == NHHT - 1))
                        m7c = C_["m7"].inc(mm_i)
                        sc.wait_ge(C_["m7"].sem, m7c)
                        dst = (s_oe[rr2] if evac_hbm else s_oesr[rr2])
                        oe_bank_free[par] = C_["oev"].inc(sc.copy(
                            dst[:, hc * 512:(hc + 1) * 512], pb[4 + par][:, :]))
                    if evac_hbm:
                        sy.wait_ge(C_["oev"].sem, C_["oev"].n)
                        oe_ring_free[rr2] = C_["oewA" if rr2 == 0 else "oewB"].inc(
                            sy.dma_start(
                                out=out_stage[tile * 128:(tile + 1) * 128, :],
                                in_=s_oe[rr2][:, :]), 16)
                    else:
                        oes_ready[tile] = C_["oev"].n
                    if post_q is not None and tile >= 1:
                        post_q(tile - 1)
                    if build_next is not None and qq + 1 < NQ:
                        r = build_next(qq + 1, kk)
                        if kk == 3:
                            ready[qq + 1] = r
                    elif tail_build is not None and qq + 1 == NQ:
                        r = tail_build(0, kk)
                        if kk == 3:
                            tail_ready[0] = r
            if post_q is not None:
                post_q(4 * NQ - 1)
            return tail_ready[0]

        for _k in range(4):
            _r = gather_q(0, _k)
        ready5 = {0: _r}
        shared_r0 = ffn_phase(w1b, w3b, w2b, oe_stage, ready5, gather_q,
                              evac_hbm=True, tail_build=build_x1q, pre=pre_exp)
        if debug:
            gp.wait_ge(C_["oewA"].sem, C_["oewA"].n)
            gp.wait_ge(C_["oewB"].sem, C_["oewB"].n)
            C_["dbgg"].inc(gp.dma_start(out=dbg["oe"][:, :], in_=oe_stage[:, :]), 16)
            C_["dbgg"].inc(gp.dma_start(out=dbg["xd"][:, :], in_=x1_all[0:128, :]), 16)
        if stop_after == "P5":
            nc.all_engine_barrier()
            p4ctx.close()
            _finish(nc, sy, gp, C_)
            return nc, dbg

        gp.wait_ge(C_["oewA"].sem, C_["oewA"].n)
        gp.wait_ge(C_["oewB"].sem, C_["oewB"].n)
        C_["agoe"].inc(gp.collective_compute(
            "AllGather", A.bypass, replica_groups=[list(range(8))],
            ins=[oe_stage.ap().opt()], outs=[oe_all.ap().opt()]))

        ready4 = {0: shared_r0}
        ffn_phase(sw1b, sw3b, sw2b, None, ready4, build_x1q,
                  evac_hbm=False, post_q=post_tile, w2_wait_m7=True,
                  last_cast=True)

        sy.wait_ge(C_["ywA"].sem, C_["ywA"].n)
        sy.wait_ge(C_["ywB"].sem, C_["ywB"].n)
        if debug:
            sy.wait_ge(C_["dbgw"].sem, C_["dbgw"].n)
            gp.wait_ge(C_["dbgg"].sem, C_["dbgg"].n)
        p4ctx.close()
    return nc, dbg


def _finish(nc, sy, gp, C_):
    sy.wait_ge(C_["dbgw"].sem, C_["dbgw"].n)
    gp.wait_ge(C_["dbgg"].sem, C_["dbgg"].n)
    sy.wait_ge(C_["ywA"].sem, C_["ywA"].n)
    sy.wait_ge(C_["ywB"].sem, C_["ywB"].n)


_CACHE = {}


def _prep(inputs):
    x = np.ascontiguousarray(np.asarray(inputs["L0_attn_query"], dtype=np.float32))
    Wqkv = np.asarray(inputs["L0_attn_Wqkv"], dtype=np.float32)
    Wproj = np.asarray(inputs["L0_attn_Wproj"], dtype=np.float32)
    bproj = np.asarray(inputs["L0_attn_bproj"], dtype=np.float32)
    Wg = np.asarray(inputs["L0_ffn_Wg"], dtype=np.float32)
    W1 = np.asarray(inputs["L0_ffn_W1"], dtype=np.float32)
    W3 = np.asarray(inputs["L0_ffn_W3"], dtype=np.float32)
    W2 = np.asarray(inputs["L0_ffn_W2"], dtype=np.float32)
    sw1 = np.asarray(inputs["L0_ffn_sw1"], dtype=np.float32)
    sw3 = np.asarray(inputs["L0_ffn_sw3"], dtype=np.float32)
    sw2 = np.asarray(inputs["L0_ffn_sw2"], dtype=np.float32)

    xT = np.ascontiguousarray(x.transpose(0, 2, 1))
    wprojT_t = np.ascontiguousarray(Wproj.T.reshape(NHT, 128, H))
    wproj_nat_t = np.ascontiguousarray(Wproj.reshape(NHT, 128, H))
    bprojr = np.ascontiguousarray(bproj.reshape(1, H))
    bprojc = np.ascontiguousarray(bproj.reshape(NHT, 128).T)
    wg_t = np.ascontiguousarray(
        Wg.reshape(NHT, 128, E).transpose(1, 0, 2).reshape(128, NHT * E))
    sw1_t = np.ascontiguousarray(
        sw1.reshape(NHT, 128, NHHT, 128).transpose(2, 1, 0, 3).reshape(NHHT, 128, NHT * 128))
    sw3_t = np.ascontiguousarray(
        sw3.reshape(NHT, 128, NHHT, 128).transpose(2, 1, 0, 3).reshape(NHHT, 128, NHT * 128))
    sw2_t = np.ascontiguousarray(sw2.reshape(NHHT, 128, H))

    per_core = []
    for c in range(8):
        blocks = []
        for hd in (c, c + 8):
            base = hd * 3 * HD
            blocks.append(Wqkv[base:base + HD])
            blocks.append(Wqkv[base + HD:base + 2 * HD])
        for hd in (c, c + 8):
            base = hd * 3 * HD
            blocks.append(Wqkv[base + 2 * HD:base + 3 * HD])
        sub = np.stack(blocks, 0)
        wqkv_c = np.ascontiguousarray(
            sub.reshape(6, 128, NHT, 128).transpose(3, 2, 0, 1).reshape(128, NHT * 6 * 128))
        xTc_c = np.ascontiguousarray(
            x[c].reshape(NST, 128, NHT, 128).transpose(0, 3, 2, 1).reshape(NST, 128, H))
        w1_c = np.ascontiguousarray(
            W1[c].reshape(NHT, 128, NHHT, 128).transpose(2, 1, 0, 3).reshape(NHHT, 128, NHT * 128))
        w3_c = np.ascontiguousarray(
            W3[c].reshape(NHT, 128, NHHT, 128).transpose(2, 1, 0, 3).reshape(NHHT, 128, NHT * 128))
        w2_c = np.ascontiguousarray(W2[c].reshape(NHHT, 128, H))
        sel = np.zeros((128, NST), dtype=np.float32)
        for i in range(NST):
            sel[16 * c + i, i] = 1.0
        per_core.append({
            "xT": xT, "x_nat": np.ascontiguousarray(x[c]), "xTc": xTc_c,
            "wqkv": wqkv_c, "wprojT": wprojT_t, "wproj_nat": wproj_nat_t,
            "bprojr": bprojr, "bprojc": bprojc, "wg": wg_t,
            "w1": w1_c, "w3": w3_c, "w2": w2_c,
            "sw1": sw1_t, "sw3": sw3_t, "sw2": sw2_t,
            "sel16": sel, "myexp": np.full((128, 1), float(c), dtype=np.float32),
        })
    return per_core


def kernel(**inputs):
    per_core = _prep(inputs)
    if "nc" not in _CACHE:
        _CACHE["nc"] = build(debug=False)[0]
    nc = _CACHE["nc"]
    res = run_bass_kernel_spmd(nc, per_core, core_ids=list(range(8)))
    return np.stack([res.results[c]["y"] for c in range(8)], axis=0)
